# version 05
# baseline (speedup 1.0000x reference)
import contextlib
import numpy as np
import concourse.bass as bass
import concourse.mybir as mybir
from concourse.bass_utils import run_bass_kernel_spmd

F32 = mybir.dt.float32
BF16 = mybir.dt.bfloat16
I8 = mybir.dt.int8
AF = mybir.ActivationFunctionType
OP = mybir.AluOpType
AX = mybir.AxisListType

B, L, DM = 4, 8192, 1024
DI, DS, DR, KC = 2048, 16, 64, 4
DH = DI // 2
NDG = DH // 128
PT = 1024
NPT = L // PT
QF = 2048
NQ = L // QF
LH = L // 2


class Sched:
    STREAMS = ("pe", "act", "dve", "pool", "qld")

    def __init__(self):
        self.ops = []
        self.per = {s: [] for s in self.STREAMS}
        self.cum = {}
        self.stream_of = {}
        self.bw = {}
        self.br = {}

    SERIALIZE = ()

    def op(self, stream, fn, reads=(), writes=(), extra=(), inc=1):
        deps = [d for d in extra if d is not None]
        if (stream in Sched.SERIALIZE or "all" in Sched.SERIALIZE) and self.ops:
            deps.append(len(self.ops) - 1)
        for b in reads:
            deps += self.bw.get(b, [])
        for b in writes:
            deps += self.bw.get(b, []) + self.br.get(b, [])
        oid = len(self.ops)
        self.ops.append((fn, deps, inc))
        prev = self.per[stream][-1] if self.per[stream] else None
        self.cum[oid] = (self.cum[prev] if prev is not None else 0) + inc
        self.per[stream].append(oid)
        self.stream_of[oid] = stream
        for b in reads:
            self.br.setdefault(b, []).append(oid)
        for b in writes:
            self.bw[b] = [oid]
            self.br[b] = []
        return oid

    def dma(self, stream, fn, reads=(), writes=(), extra=()):
        return self.op(stream, fn, reads, writes, extra, inc=16)

    def set_writer(self, buf, oid):
        self.bw[buf] = [oid]
        self.br[buf] = []

    def fence(self, stream, fn, bufs):
        oid = self.dma(stream, fn)
        for b in bufs:
            self.set_writer(b, oid)
        return oid

    def emit(self, nc, sems):
        def emit_stream(stream, eng):
            hw = {s: 0 for s in self.STREAMS}
            for oid in self.per[stream]:
                fn, deps, inc = self.ops[oid]
                need = {}
                for d in deps:
                    ds = self.stream_of[d]
                    v = self.cum[d]
                    if ds == stream and v >= self.cum[oid]:
                        continue
                    if v > hw[ds]:
                        need[ds] = max(need.get(ds, 0), v)
                for ds in sorted(need):
                    eng.wait_ge(sems[ds], need[ds])
                    hw[ds] = need[ds]
                fn(eng).then_inc(sems[stream], inc)

        with nc.Block() as block:
            @block.tensor
            def _(eng):
                emit_stream("pe", eng)

            @block.scalar
            def _(eng):
                emit_stream("act", eng)

            @block.vector
            def _(eng):
                emit_stream("dve", eng)

            @block.gpsimd
            def _(eng):
                emit_stream("pool", eng)

            @block.sync
            def _(eng):
                emit_stream("qld", eng)


def build_l1():
    nc = bass.Bass("TRN2", target_bir_lowering=False, debug=False)
    x_tok = nc.dram_tensor("x_tok", [L, DM], F32, kind="ExternalInput").ap()
    x_feat = nc.dram_tensor("x_feat", [DM, L], F32, kind="ExternalInput").ap()
    w_in_T = nc.dram_tensor("w_in_T", [DM, 2 * DH], F32, kind="ExternalInput").ap()
    conv_w_s = nc.dram_tensor("conv_w_s", [128, NDG * KC], F32, kind="ExternalInput").ap()
    conv_b_s = nc.dram_tensor("conv_b_s", [128, NDG], F32, kind="ExternalInput").ap()
    u_out = nc.dram_tensor("u_out", [DH, L], BF16, kind="ExternalOutput").ap()
    g_out = nc.dram_tensor("g_out", [DH, L], BF16, kind="ExternalOutput").ap()
    xq_out = nc.dram_tensor("xq_out", [DM, L], BF16, kind="ExternalOutput").ap()
    xs_out = nc.dram_tensor("xs_out", [1, L], F32, kind="ExternalOutput").ap()
    dr = lambda n, s, dt=F32: nc.dram_tensor(n, s, dt, kind="Internal").ap()
    st_nmr = dr("st_nmr", [1, L])
    st_rst = dr("st_rst", [1, L])
    st_ixs = dr("st_ixs", [1, L])

    sch = Sched()
    es = contextlib.ExitStack()
    with es:
        sb = lambda name, shape, dt=F32: es.enter_context(nc.sbuf_tensor(name, shape, dt))
        ps = lambda name, shape, dt=F32: es.enter_context(nc.psum_tensor(name, shape, dt))
        sems = {s: es.enter_context(nc.semaphore("sem_" + s)) for s in Sched.STREAMS}

        s_stage = sb("s_stage", [128, 2048])
        s_win_b = sb("s_win_b", [128, 8 * 2048], BF16)
        s_convw = sb("s_convw", [128, NDG * KC])
        s_convb = sb("s_convb", [128, NDG])
        s_xt = sb("s_xt", [128, 2 * DM])
        s_sum = sb("s_sum", [128, 8])
        s_sumsq = sb("s_sumsq", [128, 8])
        s_mx = sb("s_mx", [128, 8])
        s_mn = sb("s_mn", [128, 8])
        s_m = sb("s_m", [128, 8])
        s_var = sb("s_var", [128, 8])
        s_rstd = sb("s_rstd", [128, 8])
        s_nmr = sb("s_nmr", [128, 8])
        s_amx = sb("s_amx", [128, 8])
        s_xs = sb("s_xs", [128, 8])
        s_trash = sb("s_trash", [128, DM])
        s_xf = sb("s_xf", [128, 2 * PT])
        s_mrep = sb("s_mrep", [128, PT])
        s_rrep = sb("s_rrep", [128, PT])
        s_qrep = sb("s_qrep", [128, PT])
        s_t0 = sb("s_t0", [128, PT])
        s_s1 = sb("s_s1", [128, PT])
        s_xq8 = sb("s_xq8", [128, PT], I8)
        s_xnb = sb("s_xnb", [128, 8 * PT], BF16)
        s_xqb = sb("s_xqb", [128, PT], BF16)
        s_xpe = sb("s_xpe", [128, PT + 3])
        s_halo = sb("s_halo", [128, NDG * 3])
        s_conv = sb("s_conv", [128, PT])
        s_ub = sb("s_ub", [128, PT], BF16)
        s_zb = sb("s_zb", [128, PT], BF16)
        s_const = sb("s_const", [128, 2])
        s_fb1 = sb("s_fb1", [1, 16])
        s_fp1 = sb("s_fp1", [1, 16])
        nc.const_aps.register(F32, 1e-5, s_const[:, 0:1])
        p_in = [ps("p_in0", [128, PT]), ps("p_in1", [128, PT])]

        sch.op("dve", lambda v: v.memset(s_const[:, 0:1], 1e-5), writes=["consts"])
        sch.op("dve", lambda v: v.memset(s_halo[:], 0.0), writes=["halo"])

        for i in range(8):
            sch.dma("qld", (lambda q, i=i: q.dma_start(
                out=s_stage[:], in_=w_in_T[i * 128:(i + 1) * 128, :])),
                reads=["stage"], writes=["stage"])
            sch.fence("qld", lambda q: q.dma_start(out=s_fb1[:], in_=x_tok[0:1, 0:16]), ["stage"])
            sch.op("dve", (lambda v, i=i: v.tensor_copy(
                out=s_win_b[:, i * 2048:(i + 1) * 2048], in_=s_stage[:])),
                reads=["stage"], writes=["win"])
        sch.dma("qld", lambda q: q.dma_start(out=s_convw[:], in_=conv_w_s), writes=["convw"])
        sch.dma("qld", lambda q: q.dma_start(out=s_convb[:], in_=conv_b_s), writes=["convb"])
        sch.fence("qld", lambda q: q.dma_start(out=s_fb1[:], in_=x_tok[0:1, 0:16]),
                  ["convw", "convb"])

        for t in range(NPT):
            tok0 = t * PT
            for ssub in range(8):
                slot = ssub % 2
                xts = s_xt[:, slot * DM:(slot + 1) * DM]
                xb = "xt%d" % slot
                sch.dma("qld", (lambda q, t=t, ssub=ssub, xts=xts: q.dma_start(
                    out=xts, in_=x_tok[(t * 8 + ssub) * 128:(t * 8 + ssub + 1) * 128, :])),
                    writes=[xb])
                sch.fence("qld", lambda q: q.dma_start(out=s_fb1[:], in_=x_tok[0:1, 0:16]), [xb])
                sch.op("act", (lambda s, xts=xts, ssub=ssub: s.activation(
                    out=s_trash[:], in_=xts, func=AF.Copy,
                    accum_out=s_sum[:, ssub:ssub + 1])), reads=[xb], writes=["sum"])
                sch.op("act", (lambda s, xts=xts, ssub=ssub: s.activation(
                    out=s_trash[:], in_=xts, func=AF.Square,
                    accum_out=s_sumsq[:, ssub:ssub + 1])), reads=[xb], writes=["sumsq"])
                sch.op("dve", (lambda v, xts=xts, ssub=ssub: v.tensor_reduce(
                    out=s_mx[:, ssub:ssub + 1], in_=xts, axis=AX.X, op=OP.max)),
                    reads=[xb], writes=["mx"])
                sch.op("dve", (lambda v, xts=xts, ssub=ssub: v.tensor_reduce(
                    out=s_mn[:, ssub:ssub + 1], in_=xts, axis=AX.X, op=OP.min)),
                    reads=[xb], writes=["mn"])
            sch.op("dve", lambda v: v.tensor_scalar(out=s_m[:], in0=s_sum[:],
                   scalar1=1.0 / DM, scalar2=None, op0=OP.mult),
                   reads=["sum"], writes=["m"])
            sch.op("dve", lambda v: v.tensor_tensor(out=s_var[:], in0=s_m[:],
                   in1=s_m[:], op=OP.mult), reads=["m"], writes=["var"])
            sch.op("dve", lambda v: v.scalar_tensor_tensor(out=s_var[:], in0=s_sumsq[:],
                   scalar=1.0 / DM, in1=s_var[:], op0=OP.mult, op1=OP.subtract),
                   reads=["sumsq", "var"], writes=["var"])
            sch.op("act", lambda s: s.activation(out=s_rstd[:], in_=s_var[:],
                   func=AF.Sqrt, bias=1e-5), reads=["var", "consts"], writes=["rstd"])
            sch.op("dve", lambda v: v.reciprocal(out=s_rstd[:], in_=s_rstd[:]),
                   reads=["rstd"], writes=["rstd"])
            sch.op("dve", lambda v: v.scalar_tensor_tensor(out=s_nmr[:], in0=s_m[:],
                   scalar=-1.0, in1=s_rstd[:], op0=OP.mult, op1=OP.mult),
                   reads=["m", "rstd"], writes=["nmr"])
            sch.op("dve", lambda v: v.tensor_sub(out=s_mx[:], in0=s_mx[:], in1=s_m[:]),
                   reads=["mx", "m"], writes=["mx"])
            sch.op("dve", lambda v: v.tensor_sub(out=s_mn[:], in0=s_m[:], in1=s_mn[:]),
                   reads=["mn", "m"], writes=["mn"])
            sch.op("dve", lambda v: v.tensor_max(out=s_amx[:], in0=s_mx[:], in1=s_mn[:]),
                   reads=["mx", "mn"], writes=["amx"])
            sch.op("dve", lambda v: v.tensor_tensor(out=s_amx[:], in0=s_amx[:],
                   in1=s_rstd[:], op=OP.mult), reads=["amx", "rstd"], writes=["amx"])
            sch.op("dve", lambda v: v.tensor_scalar(out=s_xs[:], in0=s_amx[:],
                   scalar1=1e-6, scalar2=1.0 / 127.0, op0=OP.max, op1=OP.mult),
                   reads=["amx"], writes=["xs"])
            sch.op("dve", lambda v: v.reciprocal(out=s_amx[:], in_=s_xs[:]),
                   reads=["xs"], writes=["amx"])
            for ssub in range(8):
                o0 = tok0 + ssub * 128
                sch.dma("pool", (lambda q, ssub=ssub, o0=o0: q.dma_start(
                    out=st_nmr[0:1, o0:o0 + 128], in_=s_nmr[:, ssub:ssub + 1])),
                    reads=["nmr"], writes=["st_nmr"])
                sch.dma("pool", (lambda q, ssub=ssub, o0=o0: q.dma_start(
                    out=st_rst[0:1, o0:o0 + 128], in_=s_rstd[:, ssub:ssub + 1])),
                    reads=["rstd"], writes=["st_rst"])
                sch.dma("pool", (lambda q, ssub=ssub, o0=o0: q.dma_start(
                    out=st_ixs[0:1, o0:o0 + 128], in_=s_amx[:, ssub:ssub + 1])),
                    reads=["amx"], writes=["st_ixs"])
                sch.dma("pool", (lambda q, ssub=ssub, o0=o0: q.dma_start(
                    out=xs_out[0:1, o0:o0 + 128], in_=s_xs[:, ssub:ssub + 1])),
                    reads=["xs"], writes=["xs_d"])

            sch.fence("pool", lambda q: q.dma_start(out=s_fp1[:], in_=st_nmr[0:1, 0:16]),
                      ["st_nmr", "st_rst", "st_ixs", "xs_d"])
            sch.dma("qld", (lambda q, tok0=tok0: q.dma_start(
                out=s_mrep[:], in_=st_nmr[0:1, tok0:tok0 + PT].partition_broadcast(128))),
                reads=["st_nmr"], writes=["mrep"])
            sch.dma("qld", (lambda q, tok0=tok0: q.dma_start(
                out=s_rrep[:], in_=st_rst[0:1, tok0:tok0 + PT].partition_broadcast(128))),
                reads=["st_rst"], writes=["rrep"])
            sch.dma("qld", (lambda q, tok0=tok0: q.dma_start(
                out=s_qrep[:], in_=st_ixs[0:1, tok0:tok0 + PT].partition_broadcast(128))),
                reads=["st_ixs"], writes=["qrep"])
            for dmt in range(8):
                xsl = dmt % 2
                xfb = "xf%d" % xsl
                sch.dma("qld", (lambda q, dmt=dmt, tok0=tok0, xsl=xsl: q.dma_start(
                    out=s_xf[:, xsl * PT:(xsl + 1) * PT],
                    in_=x_feat[dmt * 128:(dmt + 1) * 128, tok0:tok0 + PT])),
                    writes=[xfb])
                sch.fence("qld", lambda q: q.dma_start(out=s_fb1[:], in_=x_tok[0:1, 0:16]), [xfb])
                sch.op("dve", (lambda v, xsl=xsl: v.tensor_mul(
                    out=s_t0[:], in0=s_xf[:, xsl * PT:(xsl + 1) * PT], in1=s_rrep[:])),
                    reads=[xfb, "rrep"], writes=["t0"])
                sch.op("dve", (lambda v: v.tensor_add(
                    out=s_s1[:], in0=s_t0[:], in1=s_mrep[:])),
                    reads=["t0", "mrep"], writes=["s1"])
                sch.op("dve", (lambda v, dmt=dmt: v.tensor_copy(
                    out=s_xnb[:, dmt * PT:(dmt + 1) * PT], in_=s_s1[:])),
                    reads=["s1"], writes=["xnb%d" % dmt])
                sch.op("dve", (lambda v: v.tensor_mul(
                    out=s_xq8[:], in0=s_s1[:], in1=s_qrep[:])),
                    reads=["s1", "qrep"], writes=["xq8"])
                sch.op("act", (lambda s: s.copy(
                    out=s_xqb[:], in_=s_xq8[:])),
                    reads=["xq8"], writes=["xqb"])
                sch.dma("pool", (lambda q, dmt=dmt, tok0=tok0: q.dma_start(
                    out=xq_out[dmt * 128:(dmt + 1) * 128, tok0:tok0 + PT], in_=s_xqb[:])),
                    reads=["xqb"])

            for rt in range(16):
                pb = rt % 2
                pbuf = "pin%d" % pb
                first = None
                for kt in range(8):
                    for hf in range(2):
                        first = sch.op("pe", (lambda p, rt=rt, kt=kt, pb=pb, hf=hf: p.matmul(
                            out=p_in[pb][:, hf * 512:(hf + 1) * 512],
                            lhsT=s_win_b[:, kt * 2048 + rt * 128: kt * 2048 + (rt + 1) * 128],
                            rhs=s_xnb[:, kt * PT + hf * 512: kt * PT + (hf + 1) * 512],
                            start=(kt == 0), stop=(kt == 7))),
                            reads=["xnb%d" % kt, "win"],
                            writes=([pbuf] if kt == 0 and hf == 0 else []))
                gid = sch.op("pe", lambda p: p.drain(),
                             reads=["xnb%d" % k2 for k2 in range(8)] + ["win"])
                sch.set_writer(pbuf, gid)
                if rt < 8:
                    dg = rt
                    sch.op("dve", (lambda v, pb=pb: v.tensor_copy(
                        out=s_xpe[:, 3:PT + 3], in_=p_in[pb][:])),
                        reads=[pbuf], writes=["xpe"])
                    sch.op("dve", (lambda v, dg=dg: v.tensor_copy(
                        out=s_xpe[:, 0:3], in_=s_halo[:, dg * 3:dg * 3 + 3])),
                        reads=["halo"], writes=["xpe_h"])
                    sch.op("dve", (lambda v, dg=dg: v.tensor_copy(
                        out=s_halo[:, dg * 3:dg * 3 + 3], in_=s_xpe[:, PT:PT + 3])),
                        reads=["xpe"], writes=["halo"])
                    sch.op("dve", (lambda v, dg=dg: v.tensor_scalar(
                        out=s_conv[:], in0=s_xpe[:, 0:PT],
                        scalar1=s_convw[:, dg * KC + 0: dg * KC + 1], scalar2=None,
                        op0=OP.mult)),
                        reads=["xpe", "xpe_h", "convw"], writes=["conv"])
                    sch.op("dve", (lambda v, dg=dg: v.scalar_tensor_tensor(
                        out=s_conv[:], in0=s_xpe[:, 1:PT + 1],
                        scalar=s_convw[:, dg * KC + 1: dg * KC + 2], in1=s_conv[:],
                        op0=OP.mult, op1=OP.add)),
                        reads=["xpe", "xpe_h", "conv", "convw"], writes=["conv"])
                    sch.op("dve", (lambda v, dg=dg: v.scalar_tensor_tensor(
                        out=s_conv[:], in0=s_xpe[:, 2:PT + 2],
                        scalar=s_convw[:, dg * KC + 2: dg * KC + 3], in1=s_conv[:],
                        op0=OP.mult, op1=OP.add)),
                        reads=["xpe", "xpe_h", "conv", "convw"], writes=["conv"])
                    sch.op("dve", (lambda v, dg=dg: v.scalar_tensor_tensor(
                        out=s_conv[:], in0=s_xpe[:, 3:PT + 3],
                        scalar=s_convw[:, dg * KC + 3: dg * KC + 4], in1=s_conv[:],
                        op0=OP.mult, op1=OP.add)),
                        reads=["xpe", "conv", "convw"], writes=["conv"])
                    sch.op("act", (lambda s, dg=dg: s.activation(
                        out=s_ub[:], in_=s_conv[:],
                        func=AF.Silu, bias=s_convb[:, dg:dg + 1])),
                        reads=["conv", "convb"], writes=["ub"])
                    sch.dma("pool", (lambda q, dg=dg, tok0=tok0: q.dma_start(
                        out=u_out[dg * 128:(dg + 1) * 128, tok0:tok0 + PT],
                        in_=s_ub[:])),
                        reads=["ub"])
                else:
                    dg = rt - 8
                    sch.op("act", (lambda s, pb=pb: s.activation(
                        out=s_zb[:], in_=p_in[pb][:], func=AF.Silu)),
                        reads=[pbuf], writes=["zb"])
                    sch.dma("pool", (lambda q, dg=dg, tok0=tok0: q.dma_start(
                        out=g_out[dg * 128:(dg + 1) * 128, tok0:tok0 + PT],
                        in_=s_zb[:])),
                        reads=["zb"])

        sch.op("pe", lambda p: p.nop(),
               extra=[sch.per["pool"][-1], sch.per["qld"][-1]])
        sch.emit(nc, sems)
    return nc


def build_l2():
    nc = bass.Bass("TRN2", target_bir_lowering=False, debug=False)
    u_full = nc.dram_tensor("u_full", [DI, L], BF16, kind="ExternalInput").ap()
    g_half = nc.dram_tensor("g_half", [DH, L], BF16, kind="ExternalInput").ap()
    w_xp_T = nc.dram_tensor("w_xp_T", [DI, 96], F32, kind="ExternalInput").ap()
    w_dt_T = nc.dram_tensor("w_dt_T", [DR, DH], F32, kind="ExternalInput").ap()
    b_dt_s = nc.dram_tensor("b_dt_s", [128, NDG], F32, kind="ExternalInput").ap()
    a_log_s = nc.dram_tensor("a_log_s", [128, NDG * DS], F32, kind="ExternalInput").ap()
    d_par_s = nc.dram_tensor("d_par_s", [128, NDG], F32, kind="ExternalInput").ap()
    yh_out = nc.dram_tensor("yh_out", [DH, L], BF16, kind="ExternalOutput").ap()
    bc_bf = nc.dram_tensor("bc_bf", [32, L], BF16, kind="Internal").ap()
    dt_d = nc.dram_tensor("dt_d", [64, L], BF16, kind="Internal").ap()

    sch = Sched()
    es = contextlib.ExitStack()
    with es:
        sb = lambda name, shape, dt=F32: es.enter_context(nc.sbuf_tensor(name, shape, dt))
        ps = lambda name, shape, dt=F32: es.enter_context(nc.psum_tensor(name, shape, dt))
        sems = {s: es.enter_context(nc.semaphore("sem_" + s)) for s in Sched.STREAMS}

        s_stage = sb("s_stage", [128, 1024])
        s_wxp_b = sb("s_wxp_b", [128, 16 * 96], BF16)
        s_wdt_b = sb("s_wdt_b", [64, DH], BF16)
        s_bdt = sb("s_bdt", [128, NDG])
        s_bdt80 = sb("s_bdt80", [128, NDG])
        s_A = sb("s_A", [128, NDG * DS])
        s_D = sb("s_D", [128, NDG])
        s_dt_b = sb("s_dt_b", [64, QF], BF16)
        s_state = sb("s_state", [128, DS * NDG])
        s_uk = sb("s_uk", [128, PT], BF16)
        s_fence2 = sb("s_fence2", [1, 16], BF16)
        s_fa2 = sb("s_fa2", [1, 16], BF16)
        s_bcb = sb("s_bcb", [32, PT], BF16)
        s_const = sb("s_const", [128, 2])
        nc.const_aps.register(F32, 1.0, s_const[:, 1:2])
        p_xp = ps("p_xp", [96, PT])
        p_dt = ps("p_dt", [128, QF])

        sch.op("dve", lambda v: v.memset(s_const[:, 1:2], 1.0), writes=["consts"])
        st0 = sch.op("dve", lambda v: v.memset(s_state[:], 0.0), writes=["state"])
        for col in range(DS * NDG):
            sch.set_writer("state%d" % col, st0)

        for i in range(16):
            sch.dma("qld", (lambda q, i=i: q.dma_start(
                out=s_stage[:, 0:96], in_=w_xp_T[i * 128:(i + 1) * 128, :])),
                reads=["stage"], writes=["stage"])
            sch.fence("qld", lambda q: q.dma_start(out=s_fence2[:], in_=u_full[0:1, 0:16]), ["stage"])
            sch.op("dve", (lambda v, i=i: v.tensor_copy(
                out=s_wxp_b[:, i * 96:(i + 1) * 96], in_=s_stage[:, 0:96])),
                reads=["stage"], writes=["wxp"])
        sch.dma("qld", lambda q: q.dma_start(out=s_stage[0:64, 0:DH], in_=w_dt_T),
                reads=["stage"], writes=["stage"])
        sch.fence("qld", lambda q: q.dma_start(out=s_fence2[:], in_=u_full[0:1, 0:16]), ["stage"])
        sch.op("dve", lambda v: v.tensor_copy(out=s_wdt_b[:], in_=s_stage[0:64, 0:DH]),
               reads=["stage"], writes=["wdt"])
        sch.dma("qld", lambda q: q.dma_start(out=s_bdt[:], in_=b_dt_s), writes=["bdt"])
        sch.dma("qld", lambda q: q.dma_start(out=s_A[:], in_=a_log_s), writes=["A"])
        sch.dma("qld", lambda q: q.dma_start(out=s_D[:], in_=d_par_s), writes=["Dp"])
        sch.fence("qld", lambda q: q.dma_start(out=s_fence2[:], in_=u_full[0:1, 0:16]),
                  ["bdt", "A", "Dp"])
        sch.op("act", lambda s: s.activation(out=s_A[:], in_=s_A[:], func=AF.Exp),
               reads=["A"], writes=["A"])
        sch.op("dve", lambda v: v.tensor_scalar(out=s_A[:], in0=s_A[:], scalar1=-1.0,
               scalar2=None, op0=OP.mult), reads=["A"], writes=["A"])
        sch.op("dve", lambda v: v.tensor_scalar(out=s_bdt80[:], in0=s_bdt[:],
               scalar1=-80.0, scalar2=None, op0=OP.add), reads=["bdt"], writes=["bdt80"])

        for t in range(NPT):
            tok0 = t * PT
            first = None
            for kt in range(16):
                fence = sch.dma("qld", (lambda q, kt=kt, tok0=tok0: q.dma_start(
                    out=s_uk[:],
                    in_=u_full[kt * 128:(kt + 1) * 128, tok0:tok0 + PT])),
                    writes=["uk0"])
                fence = sch.dma("qld", lambda q: q.dma_start(
                    out=s_fence2[:], in_=u_full[0:1, 0:16]))
                sch.set_writer("uk0", fence)
                for hf in range(2):
                    first = sch.op("pe", (lambda p, kt=kt, hf=hf: p.matmul(
                        out=p_xp[:, hf * 512:(hf + 1) * 512],
                        lhsT=s_wxp_b[:, kt * 96:(kt + 1) * 96],
                        rhs=s_uk[:, hf * 512:(hf + 1) * 512],
                        start=(kt == 0), stop=(kt == 15))),
                        reads=["uk0", "wxp"],
                        writes=(["pxp"] if kt == 0 and hf == 0 else []))
            gid = sch.op("pe", lambda p: p.drain(), reads=["uk0", "wxp"])
            sch.set_writer("pxp", gid)
            sch.op("act", (lambda s: s.copy(
                out=s_dt_b[:, 0:PT], in_=p_xp[0:64, :])),
                reads=["pxp"], writes=["dtb"])
            sch.dma("qld", (lambda q, tok0=tok0: q.dma_start(
                out=dt_d[:, tok0:tok0 + PT], in_=s_dt_b[:, 0:PT])),
                reads=["dtb"], writes=["dt_dd"])
            sch.op("act", (lambda s: s.copy(
                out=s_bcb[:], in_=p_xp[64:96, :])),
                reads=["pxp"], writes=["bcb"])
            sch.dma("qld", (lambda q, tok0=tok0: q.dma_start(
                out=bc_bf[:, tok0:tok0 + PT], in_=s_bcb[:])),
                reads=["bcb"], writes=["bc_d"])
            sch.fence("qld", lambda q: q.dma_start(out=s_fence2[:], in_=u_full[0:1, 0:16]), ["bc_d"])

        s_delta = sb("s_delta", [128, NDG * QF], BF16)
        s_g = sb("s_g", [128, NDG * QF], BF16)
        s_y = sb("s_y", [128, NDG * QF])
        s_brep = [sb("s_brep0", [128, QF], BF16), sb("s_brep1", [128, QF], BF16)]
        s_crep = [sb("s_crep0", [128, QF], BF16), sb("s_crep1", [128, QF], BF16)]
        s_dA = [sb("s_dA0", [128, QF]), sb("s_dA1", [128, QF])]
        s_w = [sb("s_w0", [128, QF], BF16), sb("s_w1", [128, QF], BF16)]
        s_h = [sb("s_h0", [128, QF], BF16), sb("s_h1", [128, QF], BF16)]
        s_t = [sb("s_t0b", [128, QF], BF16), sb("s_t1b", [128, QF], BF16)]

        for q_ in range(NQ):
            q0 = q_ * QF
            sch.dma("qld", (lambda q, q0=q0: q.dma_start(
                out=s_dt_b[:], in_=dt_d[:, q0:q0 + QF])),
                reads=["dt_dd", "dtb"], writes=["dtb"])
            fence = sch.dma("qld", lambda q: q.dma_start(
                out=s_fence2[:], in_=u_full[0:1, 0:16]))
            sch.set_writer("dtb", fence)
            for dg in range(NDG):
                first = None
                for j in range(4):
                    first = sch.op("pe", (lambda p, dg=dg, q0=q0, j=j: p.matmul(
                        out=p_dt[:, j * 512:(j + 1) * 512],
                        lhsT=s_wdt_b[:, dg * 128:(dg + 1) * 128],
                        rhs=s_dt_b[:, j * 512:(j + 1) * 512],
                        start=True, stop=True)),
                        reads=["dtb", "wdt"], writes=(["pdt"] if j == 0 else []))
                gid = sch.op("pe", lambda p: p.drain(), reads=["dtb", "wdt"])
                sch.set_writer("pdt", gid)
                sch.op("dve", (lambda v, dg=dg: v.tensor_scalar(
                    out=s_dA[0][:], in0=p_dt[:],
                    scalar1=s_bdt[:, dg:dg + 1], scalar2=80.0,
                    op0=OP.add, op1=OP.min)),
                    reads=["pdt", "bdt"], writes=["dA0"])
                sch.op("dve", (lambda v, dg=dg: v.tensor_scalar(
                    out=s_dA[1][:], in0=p_dt[:],
                    scalar1=s_bdt80[:, dg:dg + 1], scalar2=0.0,
                    op0=OP.add, op1=OP.max)),
                    reads=["pdt", "bdt80"], writes=["dA1"])
                sch.op("act", (lambda s: s.activation(
                    out=s_dA[0][:], in_=s_dA[0][:], func=AF.Exp)),
                    reads=["dA0"], writes=["dA0"])
                sch.op("act", (lambda s: s.activation(
                    out=s_dA[0][:], in_=s_dA[0][:], func=AF.Ln, bias=1.0)),
                    reads=["dA0", "consts"], writes=["dA0"])
                sch.op("dve", (lambda v, dg=dg: v.tensor_add(
                    out=s_delta[:, dg * QF:(dg + 1) * QF], in0=s_dA[0][:], in1=s_dA[1][:])),
                    reads=["dA0", "dA1"], writes=["delta%d" % dg])
                sch.dma("qld", (lambda q, dg=dg, q0=q0: q.dma_start(
                    out=s_w[1][:], in_=u_full[dg * 128:(dg + 1) * 128, q0:q0 + QF])),
                    writes=["w1"])
                sch.fence("qld", lambda q: q.dma_start(out=s_fence2[:], in_=u_full[0:1, 0:16]), ["w1"])
                sch.op("dve", (lambda v, dg=dg: v.tensor_mul(
                    out=s_g[:, dg * QF:(dg + 1) * QF],
                    in0=s_delta[:, dg * QF:(dg + 1) * QF], in1=s_w[1][:])),
                    reads=["delta%d" % dg, "w1"], writes=["g%d" % dg])
                sch.op("dve", (lambda v, dg=dg: v.tensor_scalar(
                    out=s_y[:, dg * QF:(dg + 1) * QF], in0=s_w[1][:],
                    scalar1=s_D[:, dg:dg + 1], scalar2=None, op0=OP.mult)),
                    reads=["w1", "Dp"], writes=["y%d" % dg])
            for n in range(DS):
                bs = n % 2
                sch.dma("act", (lambda q, n=n, q0=q0, bs=bs: q.dma_start(
                    out=s_brep[bs][:],
                    in_=bc_bf[n:n + 1, q0:q0 + QF].partition_broadcast(128))),
                    reads=["bc_d"], writes=["brep%d" % bs])
                sch.dma("act", (lambda q, n=n, q0=q0, bs=bs: q.dma_start(
                    out=s_crep[bs][:],
                    in_=bc_bf[16 + n:17 + n, q0:q0 + QF].partition_broadcast(128))),
                    reads=["bc_d"], writes=["crep%d" % bs])
                sch.fence("act", lambda q: q.dma_start(out=s_fa2[:], in_=u_full[0:1, 0:16]),
                          ["brep%d" % bs, "crep%d" % bs])
                for dg in range(NDG):
                    ds = (n * NDG + dg) % 2
                    col = n * NDG + dg
                    sch.op("act", (lambda s, dg=dg, n=n, ds=ds: s.activation(
                        out=s_dA[ds][:], in_=s_delta[:, dg * QF:(dg + 1) * QF],
                        func=AF.Exp, scale=s_A[:, dg * DS + n: dg * DS + n + 1])),
                        reads=["delta%d" % dg, "A"], writes=["dA%d" % ds])
                    sch.op("dve", (lambda v, dg=dg, ds=ds, bs=bs: v.tensor_mul(
                        out=s_w[ds][:], in0=s_g[:, dg * QF:(dg + 1) * QF],
                        in1=s_brep[bs][:])),
                        reads=["g%d" % dg, "brep%d" % bs], writes=["w%d" % ds])
                    sch.op("dve", (lambda v, ds=ds, col=col: v.tensor_tensor_scan(
                        out=s_h[ds][:], data0=s_dA[ds][:], data1=s_w[ds][:],
                        initial=s_state[:, col:col + 1],
                        op0=OP.mult, op1=OP.add)),
                        reads=["dA%d" % ds, "w%d" % ds, "state%d" % col],
                        writes=["h%d" % ds])
                    sch.op("dve", (lambda v, ds=ds, col=col: v.tensor_copy(
                        out=s_state[:, col:col + 1], in_=s_h[ds][:, QF - 1:QF])),
                        reads=["h%d" % ds], writes=["state%d" % col])
                    sch.op("dve", (lambda v, ds=ds, bs=bs: v.tensor_mul(
                        out=s_t[ds][:], in0=s_h[ds][:], in1=s_crep[bs][:])),
                        reads=["h%d" % ds, "crep%d" % bs], writes=["t%d" % ds])
                    sch.op("pool", (lambda g, dg=dg, ds=ds: g.tensor_add(
                        out=s_y[:, dg * QF:(dg + 1) * QF],
                        in0=s_y[:, dg * QF:(dg + 1) * QF], in1=s_t[ds][:])),
                        reads=["t%d" % ds, "y%d" % dg], writes=["y%d" % dg])
            for dg in range(NDG):
                sch.dma("qld", (lambda q, dg=dg, q0=q0: q.dma_start(
                    out=s_w[1][:], in_=g_half[dg * 128:(dg + 1) * 128, q0:q0 + QF])),
                    writes=["w1"])
                sch.fence("qld", lambda q: q.dma_start(out=s_fence2[:], in_=u_full[0:1, 0:16]), ["w1"])
                sch.op("dve", (lambda v, dg=dg: v.tensor_mul(
                    out=s_w[0][:], in0=s_y[:, dg * QF:(dg + 1) * QF], in1=s_w[1][:])),
                    reads=["y%d" % dg, "w1"], writes=["w0"])
                sch.dma("act", (lambda q, dg=dg, q0=q0: q.dma_start(
                    out=yh_out[dg * 128:(dg + 1) * 128, q0:q0 + QF], in_=s_w[0][:])),
                    reads=["w0"], writes=["yhs"])
        sch.op("pe", lambda p: p.nop(), extra=[sch.per["act"][-1]])
        sch.emit(nc, sems)
    return nc


def build_l3():
    nc = bass.Bass("TRN2", target_bir_lowering=False, debug=False)
    yh_tok = nc.dram_tensor("yh_tok", [DI, LH], BF16, kind="ExternalInput").ap()
    xq_tok = nc.dram_tensor("xq_tok", [DM, LH], BF16, kind="ExternalInput").ap()
    x_epi = nc.dram_tensor("x_epi", [DM, LH], F32, kind="ExternalInput").ap()
    xs_tok = nc.dram_tensor("xs_tok", [1, LH], F32, kind="ExternalInput").ap()
    w_out_T = nc.dram_tensor("w_out_T", [DI, DM], F32, kind="ExternalInput").ap()
    wq_T = nc.dram_tensor("wq_T", [DM, DM], I8, kind="ExternalInput").ap()
    wsc_s = nc.dram_tensor("wsc_s", [128, 8], F32, kind="ExternalInput").ap()
    fb_s = nc.dram_tensor("fb_s", [128, 8], F32, kind="ExternalInput").ap()
    out = nc.dram_tensor("out", [DM, LH], F32, kind="ExternalOutput").ap()

    sch = Sched()
    es = contextlib.ExitStack()
    with es:
        sb = lambda name, shape, dt=F32: es.enter_context(nc.sbuf_tensor(name, shape, dt))
        ps = lambda name, shape, dt=F32: es.enter_context(nc.psum_tensor(name, shape, dt))
        sems = {s: es.enter_context(nc.semaphore("sem_" + s)) for s in Sched.STREAMS}

        s_stage = sb("s_stage", [128, 1024])
        s_stage8 = sb("s_stage8", [128, 8 * DM], I8)
        s_wout_b = sb("s_wout_b", [128, 16 * DM], BF16)
        s_wq_b = sb("s_wq_b", [128, 8 * DM], BF16)
        s_wsc = sb("s_wsc", [128, 8])
        s_fb = sb("s_fb", [128, 8])
        s_yhk = [sb("s_yhk0", [128, 16 * 512], BF16), sb("s_yhk1", [128, 16 * 512], BF16)]
        s_xqk = [sb("s_xqk0", [128, 8 * 512], BF16), sb("s_xqk1", [128, 8 * 512], BF16)]
        s_xsr = [sb("s_xsr0", [128, 512]), sb("s_xsr1", [128, 512])]
        s_sv = sb("s_sv", [128, 8 * 512])
        s_fence = sb("s_fence", [1, 16], BF16)
        s_gv = sb("s_gv", [128, 512], BF16)
        s_xv = sb("s_xv", [128, 512])
        s_ov = sb("s_ov", [128, 512])
        s_ffm = sb("s_ffm", [128, 512])
        p_o = [ps("p_o0", [128, 512]), ps("p_o1", [128, 512])]
        p_f = [ps("p_f0", [128, 512]), ps("p_f1", [128, 512])]

        for i in range(16):
            sch.dma("qld", (lambda q, i=i: q.dma_start(
                out=s_stage[:], in_=w_out_T[i * 128:(i + 1) * 128, :])),
                reads=["stage"], writes=["stage"])
            sch.fence("qld", lambda q: q.dma_start(out=s_fence[:], in_=yh_tok[0:1, 0:16]), ["stage"])
            sch.op("dve", (lambda v, i=i: v.tensor_copy(
                out=s_wout_b[:, i * DM:(i + 1) * DM], in_=s_stage[:])),
                reads=["stage"], writes=["wout"])
        sch.dma("qld", lambda q: q.dma_start(
            out=s_stage8[:].rearrange("p (a b) -> p a b", b=DM),
            in_=wq_T.rearrange("(a p) b -> p a b", p=128)), writes=["stage8"])
        sch.fence("qld", lambda q: q.dma_start(out=s_fence[:], in_=yh_tok[0:1, 0:16]), ["stage8"])
        sch.op("dve", lambda v: v.tensor_copy(out=s_wq_b[:], in_=s_stage8[:]),
               reads=["stage8"], writes=["wq"])
        sch.dma("qld", lambda q: q.dma_start(out=s_wsc[:], in_=wsc_s), writes=["wsc"])
        sch.dma("qld", lambda q: q.dma_start(out=s_fb[:], in_=fb_s), writes=["fb"])
        sch.fence("qld", lambda q: q.dma_start(out=s_fence[:], in_=yh_tok[0:1, 0:16]),
                  ["wsc", "fb"])

        odmas = []
        for lc in range(LH // 512):
            l0 = lc * 512
            ksl = lc % 2
            for kt in range(16):
                sch.dma("qld", (lambda q, kt=kt, l0=l0, ksl=ksl: q.dma_start(
                    out=s_yhk[ksl][:, kt * 512:(kt + 1) * 512],
                    in_=yh_tok[kt * 128:(kt + 1) * 128, l0:l0 + 512])),
                    writes=["yhk%d_%d" % (ksl, kt)])
            for kt in range(8):
                sch.dma("qld", (lambda q, kt=kt, l0=l0, ksl=ksl: q.dma_start(
                    out=s_xqk[ksl][:, kt * 512:(kt + 1) * 512],
                    in_=xq_tok[kt * 128:(kt + 1) * 128, l0:l0 + 512])),
                    writes=["xqk%d_%d" % (ksl, kt)])
            sch.dma("qld", (lambda q, l0=l0, ksl=ksl: q.dma_start(
                out=s_xsr[ksl][:], in_=xs_tok[0:1, l0:l0 + 512].partition_broadcast(128))),
                writes=["xsr%d" % ksl])
            fence = sch.dma("qld", lambda q: q.dma_start(
                out=s_fence[:], in_=yh_tok[0:1, 0:16]))
            for k2 in range(16):
                sch.set_writer("yhk%d_%d" % (ksl, k2), fence)
            for k2 in range(8):
                sch.set_writer("xqk%d_%d" % (ksl, k2), fence)
            sch.set_writer("xsr%d" % ksl, fence)
            for ot in range(8):
                po = ot % 2
                first = None
                for kt in range(16):
                    first = sch.op("pe", (lambda p, ot=ot, kt=kt, po=po, ksl=ksl: p.matmul(
                        out=p_o[po][:],
                        lhsT=s_wout_b[:, kt * DM + ot * 128: kt * DM + (ot + 1) * 128],
                        rhs=s_yhk[ksl][:, kt * 512:(kt + 1) * 512],
                        start=(kt == 0), stop=(kt == 15))),
                        reads=["yhk%d_%d" % (ksl, kt), "wout"],
                        writes=(["po%d" % po] if kt == 0 else []))
                gid = sch.op("pe", (lambda p: p.drain()),
                             reads=["yhk%d_%d" % (ksl, k2) for k2 in range(16)] + ["wout"])
                sch.set_writer("po%d" % po, gid)
                sch.op("act", (lambda s, po=po, ot=ot: s.copy(
                    out=s_sv[:, ot * 512:(ot + 1) * 512], in_=p_o[po][:])),
                    reads=["po%d" % po], writes=["sv%d" % ot])
                firstf = None
                for kt in range(8):
                    firstf = sch.op("pe", (lambda p, ot=ot, kt=kt, po=po, ksl=ksl: p.matmul(
                        out=p_f[po][:],
                        lhsT=s_wq_b[:, kt * DM + ot * 128: kt * DM + (ot + 1) * 128],
                        rhs=s_xqk[ksl][:, kt * 512:(kt + 1) * 512],
                        start=(kt == 0), stop=(kt == 7))),
                        reads=["xqk%d_%d" % (ksl, kt), "wq"],
                        writes=(["pf%d" % po] if kt == 0 else []))
                gidf = sch.op("pe", (lambda p: p.drain()),
                              reads=["xqk%d_%d" % (ksl, k2) for k2 in range(8)] + ["wq"])
                sch.set_writer("pf%d" % po, gidf)
                sch.op("dve", (lambda v, po=po, ksl=ksl: v.tensor_mul(
                    out=s_ffm[:], in0=p_f[po][:], in1=s_xsr[ksl][:])),
                    reads=["pf%d" % po, "xsr%d" % ksl], writes=["ffm"])
                sch.op("act", (lambda s, ot=ot: s.activation(
                    out=s_gv[:], in_=s_ffm[:], func=AF.Silu,
                    scale=s_wsc[:, ot:ot + 1], bias=s_fb[:, ot:ot + 1])),
                    reads=["ffm", "wsc", "fb"], writes=["gv"])
                sch.dma("qld", (lambda q, ot=ot, l0=l0: q.dma_start(
                    out=s_xv[:], in_=x_epi[ot * 128:(ot + 1) * 128, l0:l0 + 512])),
                    reads=["xv"], writes=["xv"])
                sch.fence("qld", lambda q: q.dma_start(out=s_fence[:], in_=yh_tok[0:1, 0:16]), ["xv"])
                sch.op("dve", (lambda v, ot=ot: v.tensor_mul(
                    out=s_ov[:], in0=s_sv[:, ot * 512:(ot + 1) * 512], in1=s_gv[:])),
                    reads=["sv%d" % ot, "gv"], writes=["ov"])
                sch.op("dve", lambda v: v.tensor_add(out=s_ov[:], in0=s_ov[:], in1=s_xv[:]),
                       reads=["ov", "xv"], writes=["ov"])
                odmas.append(sch.dma("pool", (lambda q, ot=ot, l0=l0: q.dma_start(
                    out=out[ot * 128:(ot + 1) * 128, l0:l0 + 512], in_=s_ov[:])),
                    reads=["ov"]))
        sch.op("pe", lambda p: p.nop(), extra=odmas)
        sch.emit(nc, sems)
    return nc


_CACHE = {}


def kernel(L0_attn_hidden, L0_attn_in_proj_w, L0_attn_conv_w, L0_attn_conv_b,
           L0_attn_x_proj_w, L0_attn_dt_proj_w, L0_attn_dt_proj_b, L0_attn_A_log,
           L0_attn_D_param, L0_attn_out_proj_w, L0_ffn_weight_q,
           L0_ffn_weight_scale, L0_ffn_bias):
    x = np.asarray(L0_attn_hidden, dtype=np.float32)
    Win = np.asarray(L0_attn_in_proj_w, dtype=np.float32)
    convw = np.asarray(L0_attn_conv_w, dtype=np.float32)[:, 0, :]
    convb = np.asarray(L0_attn_conv_b, dtype=np.float32)
    Wxp = np.asarray(L0_attn_x_proj_w, dtype=np.float32)
    Wdt = np.asarray(L0_attn_dt_proj_w, dtype=np.float32)
    bdt = np.asarray(L0_attn_dt_proj_b, dtype=np.float32)
    Alog = np.asarray(L0_attn_A_log, dtype=np.float32)
    Dp = np.asarray(L0_attn_D_param, dtype=np.float32)
    Wout = np.asarray(L0_attn_out_proj_w, dtype=np.float32)
    Wq = np.asarray(L0_ffn_weight_q)
    wsc = np.asarray(L0_ffn_weight_scale, dtype=np.float32)
    fb = np.asarray(L0_ffn_bias, dtype=np.float32)
    C = np.ascontiguousarray

    if "l1" not in _CACHE:
        Sched.SERIALIZE = ("qld", "dve")
        _CACHE["l1"] = build_l1()
        Sched.SERIALIZE = ()
        _CACHE["l2"] = build_l2()
        Sched.SERIALIZE = ("qld", "dve")
        _CACHE["l3"] = build_l3()
        Sched.SERIALIZE = ()

    def shp(a, k):
        n = a.shape[0] // 128
        return C(a.reshape(n, 128, *a.shape[1:]).transpose(
            (1, 0, 2) if a.ndim == 2 else (1, 0)).reshape(128, -1))

    xT = [C(x[b].T) for b in range(B)]
    m1 = []
    for c in range(8):
        b, h = c // 2, c % 2
        dsl = slice(h * DH, (h + 1) * DH)
        rows = np.r_[h * DH:(h + 1) * DH, DI + h * DH: DI + (h + 1) * DH]
        m1.append({
            "x_tok": C(x[b]),
            "x_feat": xT[b],
            "w_in_T": C(Win[rows].T),
            "conv_w_s": shp(convw[dsl], KC),
            "conv_b_s": shp(convb[dsl], 1),
        })
    r1 = run_bass_kernel_spmd(_CACHE["l1"], m1, core_ids=list(range(8)))

    m2 = []
    for c in range(8):
        b, h = c // 2, c % 2
        dsl = slice(h * DH, (h + 1) * DH)
        own = r1.results[c]["u_out"]
        other = r1.results[2 * b + (1 - h)]["u_out"]
        u_full = np.concatenate([own, other], axis=0)
        wxp_rows = np.r_[h * DH:(h + 1) * DH, (1 - h) * DH:(2 - h) * DH]
        m2.append({
            "u_full": C(u_full),
            "g_half": r1.results[c]["g_out"],
            "w_xp_T": C(Wxp[:, :].T[wxp_rows]),
            "w_dt_T": C(Wdt[dsl].T),
            "b_dt_s": shp(bdt[dsl], 1),
            "a_log_s": shp(Alog[dsl], DS),
            "d_par_s": shp(Dp[dsl], 1),
        })
    r2 = run_bass_kernel_spmd(_CACHE["l2"], m2, core_ids=list(range(8)))

    m3 = []
    for c in range(8):
        b, h = c // 2, c % 2
        tsl = slice(h * LH, (h + 1) * LH)
        yh_full = np.concatenate([r2.results[2 * b]["yh_out"],
                                  r2.results[2 * b + 1]["yh_out"]], axis=0)
        m3.append({
            "yh_tok": C(yh_full[:, tsl]),
            "xq_tok": C(r1.results[c]["xq_out"][:, tsl]),
            "x_epi": C(xT[b][:, tsl]),
            "xs_tok": C(r1.results[c]["xs_out"][:, tsl]),
            "w_out_T": C(Wout.T),
            "wq_T": C(Wq.T),
            "wsc_s": shp(wsc, 1),
            "fb_s": shp(fb, 1),
        })
    r3 = run_bass_kernel_spmd(_CACHE["l3"], m3, core_ids=list(range(8)))

    outf = np.empty((B, L, DM), dtype=np.float32)
    for c in range(8):
        b, h = c // 2, c % 2
        outf[b, h * LH:(h + 1) * LH, :] = r3.results[c]["out"].T
    return outf




# version 06
# speedup vs baseline: 1.0363x; 1.0363x over previous
import contextlib
import numpy as np
import concourse.bass as bass
import concourse.mybir as mybir
from concourse.bass_utils import run_bass_kernel_spmd

F32 = mybir.dt.float32
BF16 = mybir.dt.bfloat16
I8 = mybir.dt.int8
AF = mybir.ActivationFunctionType
OP = mybir.AluOpType
AX = mybir.AxisListType

B, L, DM = 4, 8192, 1024
DI, DS, DR, KC = 2048, 16, 64, 4
DH = DI // 2
NDG = DH // 128
PT = 1024
NPT = L // PT
QF = 2048
NQ = L // QF
LH = L // 2


class Sched:
    STREAMS = ("pe", "act", "dve", "pool", "qld")

    def __init__(self):
        self.ops = []
        self.per = {s: [] for s in self.STREAMS}
        self.cum = {}
        self.stream_of = {}
        self.bw = {}
        self.br = {}

    SERIALIZE = ()

    def op(self, stream, fn, reads=(), writes=(), extra=(), inc=1):
        deps = [d for d in extra if d is not None]
        if (stream in Sched.SERIALIZE or "all" in Sched.SERIALIZE) and self.ops:
            deps.append(len(self.ops) - 1)
        for b in reads:
            deps += self.bw.get(b, [])
        for b in writes:
            deps += self.bw.get(b, []) + self.br.get(b, [])
        oid = len(self.ops)
        self.ops.append((fn, deps, inc))
        prev = self.per[stream][-1] if self.per[stream] else None
        self.cum[oid] = (self.cum[prev] if prev is not None else 0) + inc
        self.per[stream].append(oid)
        self.stream_of[oid] = stream
        for b in reads:
            self.br.setdefault(b, []).append(oid)
        for b in writes:
            self.bw[b] = [oid]
            self.br[b] = []
        return oid

    def dma(self, stream, fn, reads=(), writes=(), extra=()):
        return self.op(stream, fn, reads, writes, extra, inc=16)

    def set_writer(self, buf, oid):
        self.bw[buf] = [oid]
        self.br[buf] = []

    def fence(self, stream, fn, bufs):
        oid = self.dma(stream, fn)
        for b in bufs:
            self.set_writer(b, oid)
        return oid

    def emit(self, nc, sems):
        def emit_stream(stream, eng):
            hw = {s: 0 for s in self.STREAMS}
            for oid in self.per[stream]:
                fn, deps, inc = self.ops[oid]
                need = {}
                for d in deps:
                    ds = self.stream_of[d]
                    v = self.cum[d]
                    if ds == stream and v >= self.cum[oid]:
                        continue
                    if v > hw[ds]:
                        need[ds] = max(need.get(ds, 0), v)
                for ds in sorted(need):
                    eng.wait_ge(sems[ds], need[ds])
                    hw[ds] = need[ds]
                fn(eng).then_inc(sems[stream], inc)

        with nc.Block() as block:
            @block.tensor
            def _(eng):
                emit_stream("pe", eng)

            @block.scalar
            def _(eng):
                emit_stream("act", eng)

            @block.vector
            def _(eng):
                emit_stream("dve", eng)

            @block.gpsimd
            def _(eng):
                emit_stream("pool", eng)

            @block.sync
            def _(eng):
                emit_stream("qld", eng)


def build_l1():
    nc = bass.Bass("TRN2", target_bir_lowering=False, debug=False)
    x_tok = nc.dram_tensor("x_tok", [L, DM], F32, kind="ExternalInput").ap()
    x_feat = nc.dram_tensor("x_feat", [DM, L], F32, kind="ExternalInput").ap()
    w_in_T = nc.dram_tensor("w_in_T", [DM, 2 * DH], F32, kind="ExternalInput").ap()
    conv_w_s = nc.dram_tensor("conv_w_s", [128, NDG * KC], F32, kind="ExternalInput").ap()
    conv_b_s = nc.dram_tensor("conv_b_s", [128, NDG], F32, kind="ExternalInput").ap()
    u_out = nc.dram_tensor("u_out", [DH, L], BF16, kind="ExternalOutput").ap()
    g_out = nc.dram_tensor("g_out", [DH, L], BF16, kind="ExternalOutput").ap()
    xq_out = nc.dram_tensor("xq_out", [DM, L], BF16, kind="ExternalOutput").ap()
    xs_out = nc.dram_tensor("xs_out", [1, L], F32, kind="ExternalOutput").ap()
    dr = lambda n, s, dt=F32: nc.dram_tensor(n, s, dt, kind="Internal").ap()
    st_nmr = dr("st_nmr", [1, L])
    st_rst = dr("st_rst", [1, L])
    st_ixs = dr("st_ixs", [1, L])

    sch = Sched()
    es = contextlib.ExitStack()
    with es:
        sb = lambda name, shape, dt=F32: es.enter_context(nc.sbuf_tensor(name, shape, dt))
        ps = lambda name, shape, dt=F32: es.enter_context(nc.psum_tensor(name, shape, dt))
        sems = {s: es.enter_context(nc.semaphore("sem_" + s)) for s in Sched.STREAMS}

        s_stage = sb("s_stage", [128, 2048])
        s_win_b = sb("s_win_b", [128, 8 * 2048], BF16)
        s_convw = sb("s_convw", [128, NDG * KC])
        s_convb = sb("s_convb", [128, NDG])
        s_xt = sb("s_xt", [128, 2 * DM])
        s_sum = sb("s_sum", [128, 8])
        s_sumsq = sb("s_sumsq", [128, 8])
        s_mx = sb("s_mx", [128, 8])
        s_mn = sb("s_mn", [128, 8])
        s_m = sb("s_m", [128, 8])
        s_var = sb("s_var", [128, 8])
        s_rstd = sb("s_rstd", [128, 8])
        s_nmr = sb("s_nmr", [128, 8])
        s_amx = sb("s_amx", [128, 8])
        s_xs = sb("s_xs", [128, 8])
        s_trash = sb("s_trash", [128, DM])
        s_xf = sb("s_xf", [128, 2 * PT])
        s_mrep = sb("s_mrep", [128, PT])
        s_rrep = sb("s_rrep", [128, PT])
        s_qrep = sb("s_qrep", [128, PT])
        s_t0 = sb("s_t0", [128, PT])
        s_s1 = sb("s_s1", [128, PT])
        s_xq8 = sb("s_xq8", [128, PT], I8)
        s_xnb = sb("s_xnb", [128, 8 * PT], BF16)
        s_xqb = sb("s_xqb", [128, PT], BF16)
        s_xpe = sb("s_xpe", [128, PT + 3])
        s_halo = sb("s_halo", [128, NDG * 3])
        s_conv = sb("s_conv", [128, PT])
        s_ub = sb("s_ub", [128, PT], BF16)
        s_zb = sb("s_zb", [128, PT], BF16)
        s_const = sb("s_const", [128, 2])
        s_fb1 = sb("s_fb1", [1, 16])
        s_fp1 = sb("s_fp1", [1, 16])
        nc.const_aps.register(F32, 1e-5, s_const[:, 0:1])
        p_in = [ps("p_in0", [128, PT]), ps("p_in1", [128, PT])]

        sch.op("dve", lambda v: v.memset(s_const[:, 0:1], 1e-5), writes=["consts"])
        sch.op("dve", lambda v: v.memset(s_halo[:], 0.0), writes=["halo"])

        for i in range(8):
            sch.dma("qld", (lambda q, i=i: q.dma_start(
                out=s_stage[:], in_=w_in_T[i * 128:(i + 1) * 128, :])),
                reads=["stage"], writes=["stage"])
            sch.fence("qld", lambda q: q.dma_start(out=s_fb1[:], in_=x_tok[0:1, 0:16]), ["stage"])
            sch.op("dve", (lambda v, i=i: v.tensor_copy(
                out=s_win_b[:, i * 2048:(i + 1) * 2048], in_=s_stage[:])),
                reads=["stage"], writes=["win"])
        sch.dma("qld", lambda q: q.dma_start(out=s_convw[:], in_=conv_w_s), writes=["convw"])
        sch.dma("qld", lambda q: q.dma_start(out=s_convb[:], in_=conv_b_s), writes=["convb"])
        sch.fence("qld", lambda q: q.dma_start(out=s_fb1[:], in_=x_tok[0:1, 0:16]),
                  ["convw", "convb"])

        for t in range(NPT):
            tok0 = t * PT
            for ssub in range(8):
                slot = ssub % 2
                xts = s_xt[:, slot * DM:(slot + 1) * DM]
                xb = "xt%d" % slot
                sch.dma("qld", (lambda q, t=t, ssub=ssub, xts=xts: q.dma_start(
                    out=xts, in_=x_tok[(t * 8 + ssub) * 128:(t * 8 + ssub + 1) * 128, :])),
                    writes=[xb])
                sch.fence("qld", lambda q: q.dma_start(out=s_fb1[:], in_=x_tok[0:1, 0:16]), [xb])
                sch.op("dve", (lambda v, xts=xts, ssub=ssub: v.tensor_reduce(
                    out=s_sum[:, ssub:ssub + 1], in_=xts, axis=AX.X, op=OP.add)),
                    reads=[xb], writes=["sum"])
                sch.op("act", (lambda s, xts=xts: s.activation(
                    out=s_trash[:], in_=xts, func=AF.Square)),
                    reads=[xb], writes=["trash"])
                sch.op("dve", (lambda v, ssub=ssub: v.tensor_reduce(
                    out=s_sumsq[:, ssub:ssub + 1], in_=s_trash[:], axis=AX.X, op=OP.add)),
                    reads=["trash"], writes=["sumsq"])
                sch.op("dve", (lambda v, xts=xts, ssub=ssub: v.tensor_reduce(
                    out=s_mx[:, ssub:ssub + 1], in_=xts, axis=AX.X, op=OP.max)),
                    reads=[xb], writes=["mx"])
                sch.op("dve", (lambda v, xts=xts, ssub=ssub: v.tensor_reduce(
                    out=s_mn[:, ssub:ssub + 1], in_=xts, axis=AX.X, op=OP.min)),
                    reads=[xb], writes=["mn"])
            sch.op("dve", lambda v: v.tensor_scalar(out=s_m[:], in0=s_sum[:],
                   scalar1=1.0 / DM, scalar2=None, op0=OP.mult),
                   reads=["sum"], writes=["m"])
            sch.op("dve", lambda v: v.tensor_tensor(out=s_var[:], in0=s_m[:],
                   in1=s_m[:], op=OP.mult), reads=["m"], writes=["var"])
            sch.op("dve", lambda v: v.scalar_tensor_tensor(out=s_var[:], in0=s_sumsq[:],
                   scalar=1.0 / DM, in1=s_var[:], op0=OP.mult, op1=OP.subtract),
                   reads=["sumsq", "var"], writes=["var"])
            sch.op("act", lambda s: s.activation(out=s_rstd[:], in_=s_var[:],
                   func=AF.Sqrt, bias=1e-5), reads=["var", "consts"], writes=["rstd"])
            sch.op("dve", lambda v: v.reciprocal(out=s_rstd[:], in_=s_rstd[:]),
                   reads=["rstd"], writes=["rstd"])
            sch.op("dve", lambda v: v.scalar_tensor_tensor(out=s_nmr[:], in0=s_m[:],
                   scalar=-1.0, in1=s_rstd[:], op0=OP.mult, op1=OP.mult),
                   reads=["m", "rstd"], writes=["nmr"])
            sch.op("dve", lambda v: v.tensor_sub(out=s_mx[:], in0=s_mx[:], in1=s_m[:]),
                   reads=["mx", "m"], writes=["mx"])
            sch.op("dve", lambda v: v.tensor_sub(out=s_mn[:], in0=s_m[:], in1=s_mn[:]),
                   reads=["mn", "m"], writes=["mn"])
            sch.op("dve", lambda v: v.tensor_max(out=s_amx[:], in0=s_mx[:], in1=s_mn[:]),
                   reads=["mx", "mn"], writes=["amx"])
            sch.op("dve", lambda v: v.tensor_tensor(out=s_amx[:], in0=s_amx[:],
                   in1=s_rstd[:], op=OP.mult), reads=["amx", "rstd"], writes=["amx"])
            sch.op("dve", lambda v: v.tensor_scalar(out=s_xs[:], in0=s_amx[:],
                   scalar1=1e-6, scalar2=1.0 / 127.0, op0=OP.max, op1=OP.mult),
                   reads=["amx"], writes=["xs"])
            sch.op("dve", lambda v: v.reciprocal(out=s_amx[:], in_=s_xs[:]),
                   reads=["xs"], writes=["amx"])
            for ssub in range(8):
                o0 = tok0 + ssub * 128
                sch.dma("pool", (lambda q, ssub=ssub, o0=o0: q.dma_start(
                    out=st_nmr[0:1, o0:o0 + 128], in_=s_nmr[:, ssub:ssub + 1])),
                    reads=["nmr"], writes=["st_nmr"])
                sch.dma("pool", (lambda q, ssub=ssub, o0=o0: q.dma_start(
                    out=st_rst[0:1, o0:o0 + 128], in_=s_rstd[:, ssub:ssub + 1])),
                    reads=["rstd"], writes=["st_rst"])
                sch.dma("pool", (lambda q, ssub=ssub, o0=o0: q.dma_start(
                    out=st_ixs[0:1, o0:o0 + 128], in_=s_amx[:, ssub:ssub + 1])),
                    reads=["amx"], writes=["st_ixs"])
                sch.dma("pool", (lambda q, ssub=ssub, o0=o0: q.dma_start(
                    out=xs_out[0:1, o0:o0 + 128], in_=s_xs[:, ssub:ssub + 1])),
                    reads=["xs"], writes=["xs_d"])

            sch.fence("pool", lambda q: q.dma_start(out=s_fp1[:], in_=st_nmr[0:1, 0:16]),
                      ["st_nmr", "st_rst", "st_ixs", "xs_d"])
            sch.dma("qld", (lambda q, tok0=tok0: q.dma_start(
                out=s_mrep[:], in_=st_nmr[0:1, tok0:tok0 + PT].partition_broadcast(128))),
                reads=["st_nmr"], writes=["mrep"])
            sch.dma("qld", (lambda q, tok0=tok0: q.dma_start(
                out=s_rrep[:], in_=st_rst[0:1, tok0:tok0 + PT].partition_broadcast(128))),
                reads=["st_rst"], writes=["rrep"])
            sch.dma("qld", (lambda q, tok0=tok0: q.dma_start(
                out=s_qrep[:], in_=st_ixs[0:1, tok0:tok0 + PT].partition_broadcast(128))),
                reads=["st_ixs"], writes=["qrep"])
            for dmt in range(8):
                xsl = dmt % 2
                xfb = "xf%d" % xsl
                sch.dma("qld", (lambda q, dmt=dmt, tok0=tok0, xsl=xsl: q.dma_start(
                    out=s_xf[:, xsl * PT:(xsl + 1) * PT],
                    in_=x_feat[dmt * 128:(dmt + 1) * 128, tok0:tok0 + PT])),
                    writes=[xfb])
                sch.fence("qld", lambda q: q.dma_start(out=s_fb1[:], in_=x_tok[0:1, 0:16]), [xfb])
                sch.op("dve", (lambda v, xsl=xsl: v.tensor_mul(
                    out=s_t0[:], in0=s_xf[:, xsl * PT:(xsl + 1) * PT], in1=s_rrep[:])),
                    reads=[xfb, "rrep"], writes=["t0"])
                sch.op("dve", (lambda v: v.tensor_add(
                    out=s_s1[:], in0=s_t0[:], in1=s_mrep[:])),
                    reads=["t0", "mrep"], writes=["s1"])
                sch.op("dve", (lambda v, dmt=dmt: v.tensor_copy(
                    out=s_xnb[:, dmt * PT:(dmt + 1) * PT], in_=s_s1[:])),
                    reads=["s1"], writes=["xnb%d" % dmt])
                sch.op("dve", (lambda v: v.tensor_mul(
                    out=s_xq8[:], in0=s_s1[:], in1=s_qrep[:])),
                    reads=["s1", "qrep"], writes=["xq8"])
                sch.op("act", (lambda s: s.copy(
                    out=s_xqb[:], in_=s_xq8[:])),
                    reads=["xq8"], writes=["xqb"])
                sch.dma("pool", (lambda q, dmt=dmt, tok0=tok0: q.dma_start(
                    out=xq_out[dmt * 128:(dmt + 1) * 128, tok0:tok0 + PT], in_=s_xqb[:])),
                    reads=["xqb"])

            for rt in range(16):
                pb = rt % 2
                pbuf = "pin%d" % pb
                first = None
                for kt in range(8):
                    for hf in range(2):
                        first = sch.op("pe", (lambda p, rt=rt, kt=kt, pb=pb, hf=hf: p.matmul(
                            out=p_in[pb][:, hf * 512:(hf + 1) * 512],
                            lhsT=s_win_b[:, kt * 2048 + rt * 128: kt * 2048 + (rt + 1) * 128],
                            rhs=s_xnb[:, kt * PT + hf * 512: kt * PT + (hf + 1) * 512],
                            start=(kt == 0), stop=(kt == 7))),
                            reads=["xnb%d" % kt, "win"],
                            writes=([pbuf] if kt == 0 and hf == 0 else []))
                gid = sch.op("pe", lambda p: p.drain(),
                             reads=["xnb%d" % k2 for k2 in range(8)] + ["win"])
                sch.set_writer(pbuf, gid)
                if rt < 8:
                    dg = rt
                    sch.op("dve", (lambda v, pb=pb: v.tensor_copy(
                        out=s_xpe[:, 3:PT + 3], in_=p_in[pb][:])),
                        reads=[pbuf], writes=["xpe"])
                    sch.op("dve", (lambda v, dg=dg: v.tensor_copy(
                        out=s_xpe[:, 0:3], in_=s_halo[:, dg * 3:dg * 3 + 3])),
                        reads=["halo"], writes=["xpe_h"])
                    sch.op("dve", (lambda v, dg=dg: v.tensor_copy(
                        out=s_halo[:, dg * 3:dg * 3 + 3], in_=s_xpe[:, PT:PT + 3])),
                        reads=["xpe"], writes=["halo"])
                    sch.op("dve", (lambda v, dg=dg: v.tensor_scalar(
                        out=s_conv[:], in0=s_xpe[:, 0:PT],
                        scalar1=s_convw[:, dg * KC + 0: dg * KC + 1], scalar2=None,
                        op0=OP.mult)),
                        reads=["xpe", "xpe_h", "convw"], writes=["conv"])
                    sch.op("dve", (lambda v, dg=dg: v.scalar_tensor_tensor(
                        out=s_conv[:], in0=s_xpe[:, 1:PT + 1],
                        scalar=s_convw[:, dg * KC + 1: dg * KC + 2], in1=s_conv[:],
                        op0=OP.mult, op1=OP.add)),
                        reads=["xpe", "xpe_h", "conv", "convw"], writes=["conv"])
                    sch.op("dve", (lambda v, dg=dg: v.scalar_tensor_tensor(
                        out=s_conv[:], in0=s_xpe[:, 2:PT + 2],
                        scalar=s_convw[:, dg * KC + 2: dg * KC + 3], in1=s_conv[:],
                        op0=OP.mult, op1=OP.add)),
                        reads=["xpe", "xpe_h", "conv", "convw"], writes=["conv"])
                    sch.op("dve", (lambda v, dg=dg: v.scalar_tensor_tensor(
                        out=s_conv[:], in0=s_xpe[:, 3:PT + 3],
                        scalar=s_convw[:, dg * KC + 3: dg * KC + 4], in1=s_conv[:],
                        op0=OP.mult, op1=OP.add)),
                        reads=["xpe", "conv", "convw"], writes=["conv"])
                    sch.op("act", (lambda s, dg=dg: s.activation(
                        out=s_ub[:], in_=s_conv[:],
                        func=AF.Silu, bias=s_convb[:, dg:dg + 1])),
                        reads=["conv", "convb"], writes=["ub"])
                    sch.dma("pool", (lambda q, dg=dg, tok0=tok0: q.dma_start(
                        out=u_out[dg * 128:(dg + 1) * 128, tok0:tok0 + PT],
                        in_=s_ub[:])),
                        reads=["ub"])
                else:
                    dg = rt - 8
                    sch.op("act", (lambda s, pb=pb: s.activation(
                        out=s_zb[:], in_=p_in[pb][:], func=AF.Silu)),
                        reads=[pbuf], writes=["zb"])
                    sch.dma("pool", (lambda q, dg=dg, tok0=tok0: q.dma_start(
                        out=g_out[dg * 128:(dg + 1) * 128, tok0:tok0 + PT],
                        in_=s_zb[:])),
                        reads=["zb"])

        sch.op("pe", lambda p: p.nop(),
               extra=[sch.per["pool"][-1], sch.per["qld"][-1]])
        sch.emit(nc, sems)
    return nc


def build_l2():
    nc = bass.Bass("TRN2", target_bir_lowering=False, debug=False)
    u_full = nc.dram_tensor("u_full", [DI, L], BF16, kind="ExternalInput").ap()
    g_half = nc.dram_tensor("g_half", [DH, L], BF16, kind="ExternalInput").ap()
    w_xp_T = nc.dram_tensor("w_xp_T", [DI, 96], F32, kind="ExternalInput").ap()
    w_dt_T = nc.dram_tensor("w_dt_T", [DR, DH], F32, kind="ExternalInput").ap()
    b_dt_s = nc.dram_tensor("b_dt_s", [128, NDG], F32, kind="ExternalInput").ap()
    a_log_s = nc.dram_tensor("a_log_s", [128, NDG * DS], F32, kind="ExternalInput").ap()
    d_par_s = nc.dram_tensor("d_par_s", [128, NDG], F32, kind="ExternalInput").ap()
    yh_out = nc.dram_tensor("yh_out", [DH, L], BF16, kind="ExternalOutput").ap()
    bc_bf = nc.dram_tensor("bc_bf", [32, L], BF16, kind="Internal").ap()
    dt_d = nc.dram_tensor("dt_d", [64, L], BF16, kind="Internal").ap()

    sch = Sched()
    es = contextlib.ExitStack()
    with es:
        sb = lambda name, shape, dt=F32: es.enter_context(nc.sbuf_tensor(name, shape, dt))
        ps = lambda name, shape, dt=F32: es.enter_context(nc.psum_tensor(name, shape, dt))
        sems = {s: es.enter_context(nc.semaphore("sem_" + s)) for s in Sched.STREAMS}

        s_stage = sb("s_stage", [128, 1024])
        s_wxp_b = sb("s_wxp_b", [128, 16 * 96], BF16)
        s_wdt_b = sb("s_wdt_b", [64, DH], BF16)
        s_bdt = sb("s_bdt", [128, NDG])
        s_bdt80 = sb("s_bdt80", [128, NDG])
        s_A = sb("s_A", [128, NDG * DS])
        s_D = sb("s_D", [128, NDG])
        s_dt_b = sb("s_dt_b", [64, QF], BF16)
        s_state = sb("s_state", [128, DS * NDG])
        s_uk = sb("s_uk", [128, PT], BF16)
        s_fence2 = sb("s_fence2", [1, 16], BF16)
        s_fa2 = sb("s_fa2", [1, 16], BF16)
        s_bcb = sb("s_bcb", [32, PT], BF16)
        s_const = sb("s_const", [128, 2])
        nc.const_aps.register(F32, 1.0, s_const[:, 1:2])
        p_xp = ps("p_xp", [96, PT])
        p_dt = ps("p_dt", [128, QF])

        sch.op("dve", lambda v: v.memset(s_const[:, 1:2], 1.0), writes=["consts"])
        st0 = sch.op("dve", lambda v: v.memset(s_state[:], 0.0), writes=["state"])
        for col in range(DS * NDG):
            sch.set_writer("state%d" % col, st0)

        for i in range(16):
            sch.dma("qld", (lambda q, i=i: q.dma_start(
                out=s_stage[:, 0:96], in_=w_xp_T[i * 128:(i + 1) * 128, :])),
                reads=["stage"], writes=["stage"])
            sch.fence("qld", lambda q: q.dma_start(out=s_fence2[:], in_=u_full[0:1, 0:16]), ["stage"])
            sch.op("dve", (lambda v, i=i: v.tensor_copy(
                out=s_wxp_b[:, i * 96:(i + 1) * 96], in_=s_stage[:, 0:96])),
                reads=["stage"], writes=["wxp"])
        sch.dma("qld", lambda q: q.dma_start(out=s_stage[0:64, 0:DH], in_=w_dt_T),
                reads=["stage"], writes=["stage"])
        sch.fence("qld", lambda q: q.dma_start(out=s_fence2[:], in_=u_full[0:1, 0:16]), ["stage"])
        sch.op("dve", lambda v: v.tensor_copy(out=s_wdt_b[:], in_=s_stage[0:64, 0:DH]),
               reads=["stage"], writes=["wdt"])
        sch.dma("qld", lambda q: q.dma_start(out=s_bdt[:], in_=b_dt_s), writes=["bdt"])
        sch.dma("qld", lambda q: q.dma_start(out=s_A[:], in_=a_log_s), writes=["A"])
        sch.dma("qld", lambda q: q.dma_start(out=s_D[:], in_=d_par_s), writes=["Dp"])
        sch.fence("qld", lambda q: q.dma_start(out=s_fence2[:], in_=u_full[0:1, 0:16]),
                  ["bdt", "A", "Dp"])
        sch.op("act", lambda s: s.activation(out=s_A[:], in_=s_A[:], func=AF.Exp),
               reads=["A"], writes=["A"])
        sch.op("dve", lambda v: v.tensor_scalar(out=s_A[:], in0=s_A[:], scalar1=-1.0,
               scalar2=None, op0=OP.mult), reads=["A"], writes=["A"])
        sch.op("dve", lambda v: v.tensor_scalar(out=s_bdt80[:], in0=s_bdt[:],
               scalar1=-80.0, scalar2=None, op0=OP.add), reads=["bdt"], writes=["bdt80"])

        for t in range(NPT):
            tok0 = t * PT
            first = None
            for kt in range(16):
                fence = sch.dma("qld", (lambda q, kt=kt, tok0=tok0: q.dma_start(
                    out=s_uk[:],
                    in_=u_full[kt * 128:(kt + 1) * 128, tok0:tok0 + PT])),
                    writes=["uk0"])
                fence = sch.dma("qld", lambda q: q.dma_start(
                    out=s_fence2[:], in_=u_full[0:1, 0:16]))
                sch.set_writer("uk0", fence)
                for hf in range(2):
                    first = sch.op("pe", (lambda p, kt=kt, hf=hf: p.matmul(
                        out=p_xp[:, hf * 512:(hf + 1) * 512],
                        lhsT=s_wxp_b[:, kt * 96:(kt + 1) * 96],
                        rhs=s_uk[:, hf * 512:(hf + 1) * 512],
                        start=(kt == 0), stop=(kt == 15))),
                        reads=["uk0", "wxp"],
                        writes=(["pxp"] if kt == 0 and hf == 0 else []))
            gid = sch.op("pe", lambda p: p.drain(), reads=["uk0", "wxp"])
            sch.set_writer("pxp", gid)
            sch.op("act", (lambda s: s.copy(
                out=s_dt_b[:, 0:PT], in_=p_xp[0:64, :])),
                reads=["pxp"], writes=["dtb"])
            sch.dma("qld", (lambda q, tok0=tok0: q.dma_start(
                out=dt_d[:, tok0:tok0 + PT], in_=s_dt_b[:, 0:PT])),
                reads=["dtb"], writes=["dt_dd"])
            sch.op("act", (lambda s: s.copy(
                out=s_bcb[:], in_=p_xp[64:96, :])),
                reads=["pxp"], writes=["bcb"])
            sch.dma("qld", (lambda q, tok0=tok0: q.dma_start(
                out=bc_bf[:, tok0:tok0 + PT], in_=s_bcb[:])),
                reads=["bcb"], writes=["bc_d"])
            sch.fence("qld", lambda q: q.dma_start(out=s_fence2[:], in_=u_full[0:1, 0:16]), ["bc_d"])

        s_delta = sb("s_delta", [128, NDG * QF], BF16)
        s_g = sb("s_g", [128, NDG * QF], BF16)
        s_y = sb("s_y", [128, NDG * QF])
        s_brep = [sb("s_brep0", [128, QF], BF16), sb("s_brep1", [128, QF], BF16)]
        s_crep = [sb("s_crep0", [128, QF], BF16), sb("s_crep1", [128, QF], BF16)]
        s_dA = [sb("s_dA0", [128, QF]), sb("s_dA1", [128, QF])]
        s_w = [sb("s_w0", [128, QF], BF16), sb("s_w1", [128, QF], BF16)]
        s_h = [sb("s_h0", [128, QF], BF16), sb("s_h1", [128, QF], BF16)]
        s_t = [sb("s_t0b", [128, QF], BF16), sb("s_t1b", [128, QF], BF16)]

        for q_ in range(NQ):
            q0 = q_ * QF
            sch.dma("qld", (lambda q, q0=q0: q.dma_start(
                out=s_dt_b[:], in_=dt_d[:, q0:q0 + QF])),
                reads=["dt_dd", "dtb"], writes=["dtb"])
            fence = sch.dma("qld", lambda q: q.dma_start(
                out=s_fence2[:], in_=u_full[0:1, 0:16]))
            sch.set_writer("dtb", fence)
            for dg in range(NDG):
                first = None
                for j in range(4):
                    first = sch.op("pe", (lambda p, dg=dg, q0=q0, j=j: p.matmul(
                        out=p_dt[:, j * 512:(j + 1) * 512],
                        lhsT=s_wdt_b[:, dg * 128:(dg + 1) * 128],
                        rhs=s_dt_b[:, j * 512:(j + 1) * 512],
                        start=True, stop=True)),
                        reads=["dtb", "wdt"], writes=(["pdt"] if j == 0 else []))
                gid = sch.op("pe", lambda p: p.drain(), reads=["dtb", "wdt"])
                sch.set_writer("pdt", gid)
                sch.op("dve", (lambda v, dg=dg: v.tensor_scalar(
                    out=s_dA[0][:], in0=p_dt[:],
                    scalar1=s_bdt[:, dg:dg + 1], scalar2=80.0,
                    op0=OP.add, op1=OP.min)),
                    reads=["pdt", "bdt"], writes=["dA0"])
                sch.op("dve", (lambda v, dg=dg: v.tensor_scalar(
                    out=s_dA[1][:], in0=p_dt[:],
                    scalar1=s_bdt80[:, dg:dg + 1], scalar2=0.0,
                    op0=OP.add, op1=OP.max)),
                    reads=["pdt", "bdt80"], writes=["dA1"])
                sch.op("act", (lambda s: s.activation(
                    out=s_dA[0][:], in_=s_dA[0][:], func=AF.Exp)),
                    reads=["dA0"], writes=["dA0"])
                sch.op("act", (lambda s: s.activation(
                    out=s_dA[0][:], in_=s_dA[0][:], func=AF.Ln, bias=1.0)),
                    reads=["dA0", "consts"], writes=["dA0"])
                sch.op("dve", (lambda v, dg=dg: v.tensor_add(
                    out=s_delta[:, dg * QF:(dg + 1) * QF], in0=s_dA[0][:], in1=s_dA[1][:])),
                    reads=["dA0", "dA1"], writes=["delta%d" % dg])
                sch.dma("qld", (lambda q, dg=dg, q0=q0: q.dma_start(
                    out=s_w[1][:], in_=u_full[dg * 128:(dg + 1) * 128, q0:q0 + QF])),
                    writes=["w1"])
                sch.fence("qld", lambda q: q.dma_start(out=s_fence2[:], in_=u_full[0:1, 0:16]), ["w1"])
                sch.op("dve", (lambda v, dg=dg: v.tensor_mul(
                    out=s_g[:, dg * QF:(dg + 1) * QF],
                    in0=s_delta[:, dg * QF:(dg + 1) * QF], in1=s_w[1][:])),
                    reads=["delta%d" % dg, "w1"], writes=["g%d" % dg])
                sch.op("dve", (lambda v, dg=dg: v.tensor_scalar(
                    out=s_y[:, dg * QF:(dg + 1) * QF], in0=s_w[1][:],
                    scalar1=s_D[:, dg:dg + 1], scalar2=None, op0=OP.mult)),
                    reads=["w1", "Dp"], writes=["y%d" % dg])
            for n in range(DS):
                bs = n % 2
                sch.dma("act", (lambda q, n=n, q0=q0, bs=bs: q.dma_start(
                    out=s_brep[bs][:],
                    in_=bc_bf[n:n + 1, q0:q0 + QF].partition_broadcast(128))),
                    reads=["bc_d"], writes=["brep%d" % bs])
                sch.dma("act", (lambda q, n=n, q0=q0, bs=bs: q.dma_start(
                    out=s_crep[bs][:],
                    in_=bc_bf[16 + n:17 + n, q0:q0 + QF].partition_broadcast(128))),
                    reads=["bc_d"], writes=["crep%d" % bs])
                sch.fence("act", lambda q: q.dma_start(out=s_fa2[:], in_=u_full[0:1, 0:16]),
                          ["brep%d" % bs, "crep%d" % bs])
                for dg in range(NDG):
                    ds = (n * NDG + dg) % 2
                    col = n * NDG + dg
                    sch.op("act", (lambda s, dg=dg, n=n, ds=ds: s.activation(
                        out=s_dA[ds][:], in_=s_delta[:, dg * QF:(dg + 1) * QF],
                        func=AF.Exp, scale=s_A[:, dg * DS + n: dg * DS + n + 1])),
                        reads=["delta%d" % dg, "A"], writes=["dA%d" % ds])
                    sch.op("dve", (lambda v, dg=dg, ds=ds, bs=bs: v.tensor_mul(
                        out=s_w[ds][:], in0=s_g[:, dg * QF:(dg + 1) * QF],
                        in1=s_brep[bs][:])),
                        reads=["g%d" % dg, "brep%d" % bs], writes=["w%d" % ds])
                    sch.op("dve", (lambda v, ds=ds, col=col: v.tensor_tensor_scan(
                        out=s_h[ds][:], data0=s_dA[ds][:], data1=s_w[ds][:],
                        initial=s_state[:, col:col + 1],
                        op0=OP.mult, op1=OP.add)),
                        reads=["dA%d" % ds, "w%d" % ds, "state%d" % col],
                        writes=["h%d" % ds])
                    sch.op("dve", (lambda v, ds=ds, col=col: v.tensor_copy(
                        out=s_state[:, col:col + 1], in_=s_h[ds][:, QF - 1:QF])),
                        reads=["h%d" % ds], writes=["state%d" % col])
                    sch.op("dve", (lambda v, ds=ds, bs=bs: v.tensor_mul(
                        out=s_t[ds][:], in0=s_h[ds][:], in1=s_crep[bs][:])),
                        reads=["h%d" % ds, "crep%d" % bs], writes=["t%d" % ds])
                    sch.op("dve", (lambda v, dg=dg, ds=ds: v.tensor_add(
                        out=s_y[:, dg * QF:(dg + 1) * QF],
                        in0=s_y[:, dg * QF:(dg + 1) * QF], in1=s_t[ds][:])),
                        reads=["t%d" % ds, "y%d" % dg], writes=["y%d" % dg])
            for dg in range(NDG):
                sch.dma("qld", (lambda q, dg=dg, q0=q0: q.dma_start(
                    out=s_w[1][:], in_=g_half[dg * 128:(dg + 1) * 128, q0:q0 + QF])),
                    writes=["w1"])
                sch.fence("qld", lambda q: q.dma_start(out=s_fence2[:], in_=u_full[0:1, 0:16]), ["w1"])
                sch.op("dve", (lambda v, dg=dg: v.tensor_mul(
                    out=s_w[0][:], in0=s_y[:, dg * QF:(dg + 1) * QF], in1=s_w[1][:])),
                    reads=["y%d" % dg, "w1"], writes=["w0"])
                sch.dma("act", (lambda q, dg=dg, q0=q0: q.dma_start(
                    out=yh_out[dg * 128:(dg + 1) * 128, q0:q0 + QF], in_=s_w[0][:])),
                    reads=["w0"], writes=["yhs"])
        sch.op("pe", lambda p: p.nop(), extra=[sch.per["act"][-1]])
        sch.emit(nc, sems)
    return nc


def build_l3():
    nc = bass.Bass("TRN2", target_bir_lowering=False, debug=False)
    yh_tok = nc.dram_tensor("yh_tok", [DI, LH], BF16, kind="ExternalInput").ap()
    xq_tok = nc.dram_tensor("xq_tok", [DM, LH], BF16, kind="ExternalInput").ap()
    x_epi = nc.dram_tensor("x_epi", [DM, LH], F32, kind="ExternalInput").ap()
    xs_tok = nc.dram_tensor("xs_tok", [1, LH], F32, kind="ExternalInput").ap()
    w_out_T = nc.dram_tensor("w_out_T", [DI, DM], F32, kind="ExternalInput").ap()
    wq_T = nc.dram_tensor("wq_T", [DM, DM], I8, kind="ExternalInput").ap()
    wsc_s = nc.dram_tensor("wsc_s", [128, 8], F32, kind="ExternalInput").ap()
    fb_s = nc.dram_tensor("fb_s", [128, 8], F32, kind="ExternalInput").ap()
    out = nc.dram_tensor("out", [DM, LH], F32, kind="ExternalOutput").ap()

    sch = Sched()
    es = contextlib.ExitStack()
    with es:
        sb = lambda name, shape, dt=F32: es.enter_context(nc.sbuf_tensor(name, shape, dt))
        ps = lambda name, shape, dt=F32: es.enter_context(nc.psum_tensor(name, shape, dt))
        sems = {s: es.enter_context(nc.semaphore("sem_" + s)) for s in Sched.STREAMS}

        s_stage = sb("s_stage", [128, 1024])
        s_stage8 = sb("s_stage8", [128, 8 * DM], I8)
        s_wout_b = sb("s_wout_b", [128, 16 * DM], BF16)
        s_wq_b = sb("s_wq_b", [128, 8 * DM], BF16)
        s_wsc = sb("s_wsc", [128, 8])
        s_fb = sb("s_fb", [128, 8])
        s_yhk = [sb("s_yhk0", [128, 16 * 512], BF16), sb("s_yhk1", [128, 16 * 512], BF16)]
        s_xqk = [sb("s_xqk0", [128, 8 * 512], BF16), sb("s_xqk1", [128, 8 * 512], BF16)]
        s_xsr = [sb("s_xsr0", [128, 512]), sb("s_xsr1", [128, 512])]
        s_sv = sb("s_sv", [128, 8 * 512])
        s_fence = sb("s_fence", [1, 16], BF16)
        s_gv = sb("s_gv", [128, 512], BF16)
        s_xv = sb("s_xv", [128, 512])
        s_ov = sb("s_ov", [128, 512])
        s_ffm = sb("s_ffm", [128, 512])
        p_o = [ps("p_o0", [128, 512]), ps("p_o1", [128, 512])]
        p_f = [ps("p_f0", [128, 512]), ps("p_f1", [128, 512])]

        for i in range(16):
            sch.dma("qld", (lambda q, i=i: q.dma_start(
                out=s_stage[:], in_=w_out_T[i * 128:(i + 1) * 128, :])),
                reads=["stage"], writes=["stage"])
            sch.fence("qld", lambda q: q.dma_start(out=s_fence[:], in_=yh_tok[0:1, 0:16]), ["stage"])
            sch.op("dve", (lambda v, i=i: v.tensor_copy(
                out=s_wout_b[:, i * DM:(i + 1) * DM], in_=s_stage[:])),
                reads=["stage"], writes=["wout"])
        sch.dma("qld", lambda q: q.dma_start(
            out=s_stage8[:].rearrange("p (a b) -> p a b", b=DM),
            in_=wq_T.rearrange("(a p) b -> p a b", p=128)), writes=["stage8"])
        sch.fence("qld", lambda q: q.dma_start(out=s_fence[:], in_=yh_tok[0:1, 0:16]), ["stage8"])
        sch.op("dve", lambda v: v.tensor_copy(out=s_wq_b[:], in_=s_stage8[:]),
               reads=["stage8"], writes=["wq"])
        sch.dma("qld", lambda q: q.dma_start(out=s_wsc[:], in_=wsc_s), writes=["wsc"])
        sch.dma("qld", lambda q: q.dma_start(out=s_fb[:], in_=fb_s), writes=["fb"])
        sch.fence("qld", lambda q: q.dma_start(out=s_fence[:], in_=yh_tok[0:1, 0:16]),
                  ["wsc", "fb"])

        odmas = []
        for lc in range(LH // 512):
            l0 = lc * 512
            ksl = lc % 2
            for kt in range(16):
                sch.dma("qld", (lambda q, kt=kt, l0=l0, ksl=ksl: q.dma_start(
                    out=s_yhk[ksl][:, kt * 512:(kt + 1) * 512],
                    in_=yh_tok[kt * 128:(kt + 1) * 128, l0:l0 + 512])),
                    writes=["yhk%d_%d" % (ksl, kt)])
            for kt in range(8):
                sch.dma("qld", (lambda q, kt=kt, l0=l0, ksl=ksl: q.dma_start(
                    out=s_xqk[ksl][:, kt * 512:(kt + 1) * 512],
                    in_=xq_tok[kt * 128:(kt + 1) * 128, l0:l0 + 512])),
                    writes=["xqk%d_%d" % (ksl, kt)])
            sch.dma("qld", (lambda q, l0=l0, ksl=ksl: q.dma_start(
                out=s_xsr[ksl][:], in_=xs_tok[0:1, l0:l0 + 512].partition_broadcast(128))),
                writes=["xsr%d" % ksl])
            fence = sch.dma("qld", lambda q: q.dma_start(
                out=s_fence[:], in_=yh_tok[0:1, 0:16]))
            for k2 in range(16):
                sch.set_writer("yhk%d_%d" % (ksl, k2), fence)
            for k2 in range(8):
                sch.set_writer("xqk%d_%d" % (ksl, k2), fence)
            sch.set_writer("xsr%d" % ksl, fence)
            for ot in range(8):
                po = ot % 2
                first = None
                for kt in range(16):
                    first = sch.op("pe", (lambda p, ot=ot, kt=kt, po=po, ksl=ksl: p.matmul(
                        out=p_o[po][:],
                        lhsT=s_wout_b[:, kt * DM + ot * 128: kt * DM + (ot + 1) * 128],
                        rhs=s_yhk[ksl][:, kt * 512:(kt + 1) * 512],
                        start=(kt == 0), stop=(kt == 15))),
                        reads=["yhk%d_%d" % (ksl, kt), "wout"],
                        writes=(["po%d" % po] if kt == 0 else []))
                gid = sch.op("pe", (lambda p: p.drain()),
                             reads=["yhk%d_%d" % (ksl, k2) for k2 in range(16)] + ["wout"])
                sch.set_writer("po%d" % po, gid)
                sch.op("act", (lambda s, po=po, ot=ot: s.copy(
                    out=s_sv[:, ot * 512:(ot + 1) * 512], in_=p_o[po][:])),
                    reads=["po%d" % po], writes=["sv%d" % ot])
                firstf = None
                for kt in range(8):
                    firstf = sch.op("pe", (lambda p, ot=ot, kt=kt, po=po, ksl=ksl: p.matmul(
                        out=p_f[po][:],
                        lhsT=s_wq_b[:, kt * DM + ot * 128: kt * DM + (ot + 1) * 128],
                        rhs=s_xqk[ksl][:, kt * 512:(kt + 1) * 512],
                        start=(kt == 0), stop=(kt == 7))),
                        reads=["xqk%d_%d" % (ksl, kt), "wq"],
                        writes=(["pf%d" % po] if kt == 0 else []))
                gidf = sch.op("pe", (lambda p: p.drain()),
                              reads=["xqk%d_%d" % (ksl, k2) for k2 in range(8)] + ["wq"])
                sch.set_writer("pf%d" % po, gidf)
                sch.op("dve", (lambda v, po=po, ksl=ksl: v.tensor_mul(
                    out=s_ffm[:], in0=p_f[po][:], in1=s_xsr[ksl][:])),
                    reads=["pf%d" % po, "xsr%d" % ksl], writes=["ffm"])
                sch.op("act", (lambda s, ot=ot: s.activation(
                    out=s_gv[:], in_=s_ffm[:], func=AF.Silu,
                    scale=s_wsc[:, ot:ot + 1], bias=s_fb[:, ot:ot + 1])),
                    reads=["ffm", "wsc", "fb"], writes=["gv"])
                sch.dma("qld", (lambda q, ot=ot, l0=l0: q.dma_start(
                    out=s_xv[:], in_=x_epi[ot * 128:(ot + 1) * 128, l0:l0 + 512])),
                    reads=["xv"], writes=["xv"])
                sch.fence("qld", lambda q: q.dma_start(out=s_fence[:], in_=yh_tok[0:1, 0:16]), ["xv"])
                sch.op("dve", (lambda v, ot=ot: v.tensor_mul(
                    out=s_ov[:], in0=s_sv[:, ot * 512:(ot + 1) * 512], in1=s_gv[:])),
                    reads=["sv%d" % ot, "gv"], writes=["ov"])
                sch.op("dve", lambda v: v.tensor_add(out=s_ov[:], in0=s_ov[:], in1=s_xv[:]),
                       reads=["ov", "xv"], writes=["ov"])
                odmas.append(sch.dma("pool", (lambda q, ot=ot, l0=l0: q.dma_start(
                    out=out[ot * 128:(ot + 1) * 128, l0:l0 + 512], in_=s_ov[:])),
                    reads=["ov"]))
        sch.op("pe", lambda p: p.nop(), extra=odmas)
        sch.emit(nc, sems)
    return nc


_CACHE = {}


def kernel(L0_attn_hidden, L0_attn_in_proj_w, L0_attn_conv_w, L0_attn_conv_b,
           L0_attn_x_proj_w, L0_attn_dt_proj_w, L0_attn_dt_proj_b, L0_attn_A_log,
           L0_attn_D_param, L0_attn_out_proj_w, L0_ffn_weight_q,
           L0_ffn_weight_scale, L0_ffn_bias):
    x = np.asarray(L0_attn_hidden, dtype=np.float32)
    Win = np.asarray(L0_attn_in_proj_w, dtype=np.float32)
    convw = np.asarray(L0_attn_conv_w, dtype=np.float32)[:, 0, :]
    convb = np.asarray(L0_attn_conv_b, dtype=np.float32)
    Wxp = np.asarray(L0_attn_x_proj_w, dtype=np.float32)
    Wdt = np.asarray(L0_attn_dt_proj_w, dtype=np.float32)
    bdt = np.asarray(L0_attn_dt_proj_b, dtype=np.float32)
    Alog = np.asarray(L0_attn_A_log, dtype=np.float32)
    Dp = np.asarray(L0_attn_D_param, dtype=np.float32)
    Wout = np.asarray(L0_attn_out_proj_w, dtype=np.float32)
    Wq = np.asarray(L0_ffn_weight_q)
    wsc = np.asarray(L0_ffn_weight_scale, dtype=np.float32)
    fb = np.asarray(L0_ffn_bias, dtype=np.float32)
    C = np.ascontiguousarray

    if "l1" not in _CACHE:
        Sched.SERIALIZE = ("qld", "dve")
        _CACHE["l1"] = build_l1()
        Sched.SERIALIZE = ()
        _CACHE["l2"] = build_l2()
        Sched.SERIALIZE = ("qld", "dve")
        _CACHE["l3"] = build_l3()
        Sched.SERIALIZE = ()

    def shp(a, k):
        n = a.shape[0] // 128
        return C(a.reshape(n, 128, *a.shape[1:]).transpose(
            (1, 0, 2) if a.ndim == 2 else (1, 0)).reshape(128, -1))

    xT = [C(x[b].T) for b in range(B)]
    m1 = []
    for c in range(8):
        b, h = c // 2, c % 2
        dsl = slice(h * DH, (h + 1) * DH)
        rows = np.r_[h * DH:(h + 1) * DH, DI + h * DH: DI + (h + 1) * DH]
        m1.append({
            "x_tok": C(x[b]),
            "x_feat": xT[b],
            "w_in_T": C(Win[rows].T),
            "conv_w_s": shp(convw[dsl], KC),
            "conv_b_s": shp(convb[dsl], 1),
        })
    r1 = run_bass_kernel_spmd(_CACHE["l1"], m1, core_ids=list(range(8)))

    m2 = []
    for c in range(8):
        b, h = c // 2, c % 2
        dsl = slice(h * DH, (h + 1) * DH)
        own = r1.results[c]["u_out"]
        other = r1.results[2 * b + (1 - h)]["u_out"]
        u_full = np.concatenate([own, other], axis=0)
        wxp_rows = np.r_[h * DH:(h + 1) * DH, (1 - h) * DH:(2 - h) * DH]
        m2.append({
            "u_full": C(u_full),
            "g_half": r1.results[c]["g_out"],
            "w_xp_T": C(Wxp[:, :].T[wxp_rows]),
            "w_dt_T": C(Wdt[dsl].T),
            "b_dt_s": shp(bdt[dsl], 1),
            "a_log_s": shp(Alog[dsl], DS),
            "d_par_s": shp(Dp[dsl], 1),
        })
    r2 = run_bass_kernel_spmd(_CACHE["l2"], m2, core_ids=list(range(8)))

    m3 = []
    for c in range(8):
        b, h = c // 2, c % 2
        tsl = slice(h * LH, (h + 1) * LH)
        yh_full = np.concatenate([r2.results[2 * b]["yh_out"],
                                  r2.results[2 * b + 1]["yh_out"]], axis=0)
        m3.append({
            "yh_tok": C(yh_full[:, tsl]),
            "xq_tok": C(r1.results[c]["xq_out"][:, tsl]),
            "x_epi": C(xT[b][:, tsl]),
            "xs_tok": C(r1.results[c]["xs_out"][:, tsl]),
            "w_out_T": C(Wout.T),
            "wq_T": C(Wq.T),
            "wsc_s": shp(wsc, 1),
            "fb_s": shp(fb, 1),
        })
    r3 = run_bass_kernel_spmd(_CACHE["l3"], m3, core_ids=list(range(8)))

    outf = np.empty((B, L, DM), dtype=np.float32)
    for c in range(8):
        b, h = c // 2, c % 2
        outf[b, h * LH:(h + 1) * LH, :] = r3.results[c]["out"].T
    return outf




# version 07
# speedup vs baseline: 1.1978x; 1.1558x over previous
import contextlib
import numpy as np
import concourse.bass as bass
import concourse.mybir as mybir
from concourse.bass_utils import run_bass_kernel_spmd

F32 = mybir.dt.float32
BF16 = mybir.dt.bfloat16
I8 = mybir.dt.int8
AF = mybir.ActivationFunctionType
OP = mybir.AluOpType
AX = mybir.AxisListType

B, L, DM = 4, 8192, 1024
DI, DS, DR, KC = 2048, 16, 64, 4
DH = DI // 2
NDG = DH // 128
PT = 1024
NPT = L // PT
QF = 2048
NQ = L // QF
LH = L // 2


class Sched:
    STREAMS = ("pe", "act", "dve", "pool", "qld")

    def __init__(self):
        self.ops = []
        self.per = {s: [] for s in self.STREAMS}
        self.cum = {}
        self.stream_of = {}
        self.bw = {}
        self.br = {}

    SERIALIZE = ()

    def op(self, stream, fn, reads=(), writes=(), extra=(), inc=1):
        deps = [d for d in extra if d is not None]
        if (stream in Sched.SERIALIZE or "all" in Sched.SERIALIZE) and self.ops:
            deps.append(len(self.ops) - 1)
        for b in reads:
            deps += self.bw.get(b, [])
        for b in writes:
            deps += self.bw.get(b, []) + self.br.get(b, [])
        oid = len(self.ops)
        self.ops.append((fn, deps, inc))
        prev = self.per[stream][-1] if self.per[stream] else None
        self.cum[oid] = (self.cum[prev] if prev is not None else 0) + inc
        self.per[stream].append(oid)
        self.stream_of[oid] = stream
        for b in reads:
            self.br.setdefault(b, []).append(oid)
        for b in writes:
            self.bw[b] = [oid]
            self.br[b] = []
        return oid

    def dma(self, stream, fn, reads=(), writes=(), extra=()):
        return self.op(stream, fn, reads, writes, extra, inc=16)

    def set_writer(self, buf, oid):
        self.bw[buf] = [oid]
        self.br[buf] = []

    def fence(self, stream, fn, bufs):
        oid = self.dma(stream, fn)
        for b in bufs:
            self.set_writer(b, oid)
        return oid

    def emit(self, nc, sems):
        def emit_stream(stream, eng):
            hw = {s: 0 for s in self.STREAMS}
            for oid in self.per[stream]:
                fn, deps, inc = self.ops[oid]
                need = {}
                for d in deps:
                    ds = self.stream_of[d]
                    v = self.cum[d]
                    if ds == stream and v >= self.cum[oid]:
                        continue
                    if ds == stream and inc == 16 and self.ops[d][2] == 16:
                        continue
                    if v > hw[ds]:
                        need[ds] = max(need.get(ds, 0), v)
                for ds in sorted(need):
                    eng.wait_ge(sems[ds], need[ds])
                    hw[ds] = need[ds]
                fn(eng).then_inc(sems[stream], inc)

        with nc.Block() as block:
            @block.tensor
            def _(eng):
                emit_stream("pe", eng)

            @block.scalar
            def _(eng):
                emit_stream("act", eng)

            @block.vector
            def _(eng):
                emit_stream("dve", eng)

            @block.gpsimd
            def _(eng):
                emit_stream("pool", eng)

            @block.sync
            def _(eng):
                emit_stream("qld", eng)


def build_l1():
    nc = bass.Bass("TRN2", target_bir_lowering=False, debug=False)
    x_tok = nc.dram_tensor("x_tok", [L, DM], F32, kind="ExternalInput").ap()
    x_feat = nc.dram_tensor("x_feat", [DM, L], F32, kind="ExternalInput").ap()
    w_in_T = nc.dram_tensor("w_in_T", [DM, 2 * DH], F32, kind="ExternalInput").ap()
    conv_w_s = nc.dram_tensor("conv_w_s", [128, NDG * KC], F32, kind="ExternalInput").ap()
    conv_b_s = nc.dram_tensor("conv_b_s", [128, NDG], F32, kind="ExternalInput").ap()
    u_out = nc.dram_tensor("u_out", [DH, L], BF16, kind="ExternalOutput").ap()
    g_out = nc.dram_tensor("g_out", [DH, L], BF16, kind="ExternalOutput").ap()
    xq_out = nc.dram_tensor("xq_out", [DM, L], BF16, kind="ExternalOutput").ap()
    xs_out = nc.dram_tensor("xs_out", [1, L], F32, kind="ExternalOutput").ap()
    dr = lambda n, s, dt=F32: nc.dram_tensor(n, s, dt, kind="Internal").ap()
    st_nmr = dr("st_nmr", [1, L])
    st_rst = dr("st_rst", [1, L])
    st_ixs = dr("st_ixs", [1, L])

    sch = Sched()
    es = contextlib.ExitStack()
    with es:
        sb = lambda name, shape, dt=F32: es.enter_context(nc.sbuf_tensor(name, shape, dt))
        ps = lambda name, shape, dt=F32: es.enter_context(nc.psum_tensor(name, shape, dt))
        sems = {s: es.enter_context(nc.semaphore("sem_" + s)) for s in Sched.STREAMS}

        s_stage = sb("s_stage", [128, 2048])
        s_win_b = sb("s_win_b", [128, 8 * 2048], BF16)
        s_convw = sb("s_convw", [128, NDG * KC])
        s_convb = sb("s_convb", [128, NDG])
        s_xt = sb("s_xt", [128, 2 * DM])
        s_sum = sb("s_sum", [128, 8])
        s_sumsq = sb("s_sumsq", [128, 8])
        s_mx = sb("s_mx", [128, 8])
        s_mn = sb("s_mn", [128, 8])
        s_m = sb("s_m", [128, 8])
        s_var = sb("s_var", [128, 8])
        s_rstd = sb("s_rstd", [128, 8])
        s_nmr = sb("s_nmr", [128, 8])
        s_amx = sb("s_amx", [128, 8])
        s_xs = sb("s_xs", [128, 8])
        s_trash = sb("s_trash", [128, DM])
        s_xf = sb("s_xf", [128, 2 * PT])
        s_mrep = sb("s_mrep", [128, PT])
        s_rrep = sb("s_rrep", [128, PT])
        s_qrep = sb("s_qrep", [128, PT])
        s_t0 = sb("s_t0", [128, PT])
        s_s1 = sb("s_s1", [128, PT])
        s_xq8 = sb("s_xq8", [128, PT], I8)
        s_xnb = sb("s_xnb", [128, 8 * PT], BF16)
        s_xqb = sb("s_xqb", [128, PT], BF16)
        s_xpe = sb("s_xpe", [128, PT + 3])
        s_halo = sb("s_halo", [128, NDG * 3])
        s_conv = sb("s_conv", [128, PT])
        s_ub = sb("s_ub", [128, PT], BF16)
        s_zb = sb("s_zb", [128, PT], BF16)
        s_const = sb("s_const", [128, 2])
        s_fb1 = sb("s_fb1", [1, 16])
        s_fp1 = sb("s_fp1", [1, 16])
        nc.const_aps.register(F32, 1e-5, s_const[:, 0:1])
        p_in = [ps("p_in0", [128, PT]), ps("p_in1", [128, PT])]

        sch.op("dve", lambda v: v.memset(s_const[:, 0:1], 1e-5), writes=["consts"])
        sch.op("dve", lambda v: v.memset(s_halo[:], 0.0), writes=["halo"])

        for i in range(8):
            sch.dma("qld", (lambda q, i=i: q.dma_start(
                out=s_stage[:], in_=w_in_T[i * 128:(i + 1) * 128, :])),
                reads=["stage"], writes=["stage"])
            sch.fence("qld", lambda q: q.dma_start(out=s_fb1[:], in_=x_tok[0:1, 0:16]), ["stage"])
            sch.op("dve", (lambda v, i=i: v.tensor_copy(
                out=s_win_b[:, i * 2048:(i + 1) * 2048], in_=s_stage[:])),
                reads=["stage"], writes=["win"])
        sch.dma("qld", lambda q: q.dma_start(out=s_convw[:], in_=conv_w_s), writes=["convw"])
        sch.dma("qld", lambda q: q.dma_start(out=s_convb[:], in_=conv_b_s), writes=["convb"])
        sch.fence("qld", lambda q: q.dma_start(out=s_fb1[:], in_=x_tok[0:1, 0:16]),
                  ["convw", "convb"])

        for t in range(NPT):
            tok0 = t * PT
            for ssub in range(8):
                slot = ssub % 2
                xts = s_xt[:, slot * DM:(slot + 1) * DM]
                xb = "xt%d" % slot
                sch.dma("qld", (lambda q, t=t, ssub=ssub, xts=xts: q.dma_start(
                    out=xts, in_=x_tok[(t * 8 + ssub) * 128:(t * 8 + ssub + 1) * 128, :])),
                    writes=[xb])
                sch.fence("qld", lambda q: q.dma_start(out=s_fb1[:], in_=x_tok[0:1, 0:16]), [xb])
                sch.op("dve", (lambda v, xts=xts, ssub=ssub: v.tensor_reduce(
                    out=s_sum[:, ssub:ssub + 1], in_=xts, axis=AX.X, op=OP.add)),
                    reads=[xb], writes=["sum"])
                sch.op("act", (lambda s, xts=xts: s.activation(
                    out=s_trash[:], in_=xts, func=AF.Square)),
                    reads=[xb], writes=["trash"])
                sch.op("dve", (lambda v, ssub=ssub: v.tensor_reduce(
                    out=s_sumsq[:, ssub:ssub + 1], in_=s_trash[:], axis=AX.X, op=OP.add)),
                    reads=["trash"], writes=["sumsq"])
                sch.op("dve", (lambda v, xts=xts, ssub=ssub: v.tensor_reduce(
                    out=s_mx[:, ssub:ssub + 1], in_=xts, axis=AX.X, op=OP.max)),
                    reads=[xb], writes=["mx"])
                sch.op("dve", (lambda v, xts=xts, ssub=ssub: v.tensor_reduce(
                    out=s_mn[:, ssub:ssub + 1], in_=xts, axis=AX.X, op=OP.min)),
                    reads=[xb], writes=["mn"])
            sch.op("dve", lambda v: v.tensor_scalar(out=s_m[:], in0=s_sum[:],
                   scalar1=1.0 / DM, scalar2=None, op0=OP.mult),
                   reads=["sum"], writes=["m"])
            sch.op("dve", lambda v: v.tensor_tensor(out=s_var[:], in0=s_m[:],
                   in1=s_m[:], op=OP.mult), reads=["m"], writes=["var"])
            sch.op("dve", lambda v: v.scalar_tensor_tensor(out=s_var[:], in0=s_sumsq[:],
                   scalar=1.0 / DM, in1=s_var[:], op0=OP.mult, op1=OP.subtract),
                   reads=["sumsq", "var"], writes=["var"])
            sch.op("act", lambda s: s.activation(out=s_rstd[:], in_=s_var[:],
                   func=AF.Sqrt, bias=1e-5), reads=["var", "consts"], writes=["rstd"])
            sch.op("dve", lambda v: v.reciprocal(out=s_rstd[:], in_=s_rstd[:]),
                   reads=["rstd"], writes=["rstd"])
            sch.op("dve", lambda v: v.scalar_tensor_tensor(out=s_nmr[:], in0=s_m[:],
                   scalar=-1.0, in1=s_rstd[:], op0=OP.mult, op1=OP.mult),
                   reads=["m", "rstd"], writes=["nmr"])
            sch.op("dve", lambda v: v.tensor_sub(out=s_mx[:], in0=s_mx[:], in1=s_m[:]),
                   reads=["mx", "m"], writes=["mx"])
            sch.op("dve", lambda v: v.tensor_sub(out=s_mn[:], in0=s_m[:], in1=s_mn[:]),
                   reads=["mn", "m"], writes=["mn"])
            sch.op("dve", lambda v: v.tensor_max(out=s_amx[:], in0=s_mx[:], in1=s_mn[:]),
                   reads=["mx", "mn"], writes=["amx"])
            sch.op("dve", lambda v: v.tensor_tensor(out=s_amx[:], in0=s_amx[:],
                   in1=s_rstd[:], op=OP.mult), reads=["amx", "rstd"], writes=["amx"])
            sch.op("dve", lambda v: v.tensor_scalar(out=s_xs[:], in0=s_amx[:],
                   scalar1=1e-6, scalar2=1.0 / 127.0, op0=OP.max, op1=OP.mult),
                   reads=["amx"], writes=["xs"])
            sch.op("dve", lambda v: v.reciprocal(out=s_amx[:], in_=s_xs[:]),
                   reads=["xs"], writes=["amx"])
            for ssub in range(8):
                o0 = tok0 + ssub * 128
                sch.dma("pool", (lambda q, ssub=ssub, o0=o0: q.dma_start(
                    out=st_nmr[0:1, o0:o0 + 128], in_=s_nmr[:, ssub:ssub + 1])),
                    reads=["nmr"], writes=["st_nmr"])
                sch.dma("pool", (lambda q, ssub=ssub, o0=o0: q.dma_start(
                    out=st_rst[0:1, o0:o0 + 128], in_=s_rstd[:, ssub:ssub + 1])),
                    reads=["rstd"], writes=["st_rst"])
                sch.dma("pool", (lambda q, ssub=ssub, o0=o0: q.dma_start(
                    out=st_ixs[0:1, o0:o0 + 128], in_=s_amx[:, ssub:ssub + 1])),
                    reads=["amx"], writes=["st_ixs"])
                sch.dma("pool", (lambda q, ssub=ssub, o0=o0: q.dma_start(
                    out=xs_out[0:1, o0:o0 + 128], in_=s_xs[:, ssub:ssub + 1])),
                    reads=["xs"], writes=["xs_d"])

            sch.fence("pool", lambda q: q.dma_start(out=s_fp1[:], in_=st_nmr[0:1, 0:16]),
                      ["st_nmr", "st_rst", "st_ixs", "xs_d"])
            sch.dma("qld", (lambda q, tok0=tok0: q.dma_start(
                out=s_mrep[:], in_=st_nmr[0:1, tok0:tok0 + PT].partition_broadcast(128))),
                reads=["st_nmr"], writes=["mrep"])
            sch.dma("qld", (lambda q, tok0=tok0: q.dma_start(
                out=s_rrep[:], in_=st_rst[0:1, tok0:tok0 + PT].partition_broadcast(128))),
                reads=["st_rst"], writes=["rrep"])
            sch.dma("qld", (lambda q, tok0=tok0: q.dma_start(
                out=s_qrep[:], in_=st_ixs[0:1, tok0:tok0 + PT].partition_broadcast(128))),
                reads=["st_ixs"], writes=["qrep"])
            for dmt in range(8):
                xsl = dmt % 2
                xfb = "xf%d" % xsl
                sch.dma("qld", (lambda q, dmt=dmt, tok0=tok0, xsl=xsl: q.dma_start(
                    out=s_xf[:, xsl * PT:(xsl + 1) * PT],
                    in_=x_feat[dmt * 128:(dmt + 1) * 128, tok0:tok0 + PT])),
                    writes=[xfb])
                sch.fence("qld", lambda q: q.dma_start(out=s_fb1[:], in_=x_tok[0:1, 0:16]), [xfb])
                sch.op("dve", (lambda v, xsl=xsl: v.tensor_mul(
                    out=s_t0[:], in0=s_xf[:, xsl * PT:(xsl + 1) * PT], in1=s_rrep[:])),
                    reads=[xfb, "rrep"], writes=["t0"])
                sch.op("dve", (lambda v: v.tensor_add(
                    out=s_s1[:], in0=s_t0[:], in1=s_mrep[:])),
                    reads=["t0", "mrep"], writes=["s1"])
                sch.op("dve", (lambda v, dmt=dmt: v.tensor_copy(
                    out=s_xnb[:, dmt * PT:(dmt + 1) * PT], in_=s_s1[:])),
                    reads=["s1"], writes=["xnb%d" % dmt])
                sch.op("dve", (lambda v: v.tensor_mul(
                    out=s_xq8[:], in0=s_s1[:], in1=s_qrep[:])),
                    reads=["s1", "qrep"], writes=["xq8"])
                sch.op("act", (lambda s: s.copy(
                    out=s_xqb[:], in_=s_xq8[:])),
                    reads=["xq8"], writes=["xqb"])
                sch.dma("pool", (lambda q, dmt=dmt, tok0=tok0: q.dma_start(
                    out=xq_out[dmt * 128:(dmt + 1) * 128, tok0:tok0 + PT], in_=s_xqb[:])),
                    reads=["xqb"])

            for rt in range(16):
                pb = rt % 2
                pbuf = "pin%d" % pb
                first = None
                for kt in range(8):
                    for hf in range(2):
                        first = sch.op("pe", (lambda p, rt=rt, kt=kt, pb=pb, hf=hf: p.matmul(
                            out=p_in[pb][:, hf * 512:(hf + 1) * 512],
                            lhsT=s_win_b[:, kt * 2048 + rt * 128: kt * 2048 + (rt + 1) * 128],
                            rhs=s_xnb[:, kt * PT + hf * 512: kt * PT + (hf + 1) * 512],
                            start=(kt == 0), stop=(kt == 7))),
                            reads=["xnb%d" % kt, "win"],
                            writes=([pbuf] if kt == 0 and hf == 0 else []))
                gid = sch.op("pe", lambda p: p.drain(),
                             reads=["xnb%d" % k2 for k2 in range(8)] + ["win"])
                sch.set_writer(pbuf, gid)
                if rt < 8:
                    dg = rt
                    sch.op("dve", (lambda v, pb=pb: v.tensor_copy(
                        out=s_xpe[:, 3:PT + 3], in_=p_in[pb][:])),
                        reads=[pbuf], writes=["xpe"])
                    sch.op("dve", (lambda v, dg=dg: v.tensor_copy(
                        out=s_xpe[:, 0:3], in_=s_halo[:, dg * 3:dg * 3 + 3])),
                        reads=["halo"], writes=["xpe_h"])
                    sch.op("dve", (lambda v, dg=dg: v.tensor_copy(
                        out=s_halo[:, dg * 3:dg * 3 + 3], in_=s_xpe[:, PT:PT + 3])),
                        reads=["xpe"], writes=["halo"])
                    sch.op("dve", (lambda v, dg=dg: v.tensor_scalar(
                        out=s_conv[:], in0=s_xpe[:, 0:PT],
                        scalar1=s_convw[:, dg * KC + 0: dg * KC + 1], scalar2=None,
                        op0=OP.mult)),
                        reads=["xpe", "xpe_h", "convw"], writes=["conv"])
                    sch.op("dve", (lambda v, dg=dg: v.scalar_tensor_tensor(
                        out=s_conv[:], in0=s_xpe[:, 1:PT + 1],
                        scalar=s_convw[:, dg * KC + 1: dg * KC + 2], in1=s_conv[:],
                        op0=OP.mult, op1=OP.add)),
                        reads=["xpe", "xpe_h", "conv", "convw"], writes=["conv"])
                    sch.op("dve", (lambda v, dg=dg: v.scalar_tensor_tensor(
                        out=s_conv[:], in0=s_xpe[:, 2:PT + 2],
                        scalar=s_convw[:, dg * KC + 2: dg * KC + 3], in1=s_conv[:],
                        op0=OP.mult, op1=OP.add)),
                        reads=["xpe", "xpe_h", "conv", "convw"], writes=["conv"])
                    sch.op("dve", (lambda v, dg=dg: v.scalar_tensor_tensor(
                        out=s_conv[:], in0=s_xpe[:, 3:PT + 3],
                        scalar=s_convw[:, dg * KC + 3: dg * KC + 4], in1=s_conv[:],
                        op0=OP.mult, op1=OP.add)),
                        reads=["xpe", "conv", "convw"], writes=["conv"])
                    sch.op("act", (lambda s, dg=dg: s.activation(
                        out=s_ub[:], in_=s_conv[:],
                        func=AF.Silu, bias=s_convb[:, dg:dg + 1])),
                        reads=["conv", "convb"], writes=["ub"])
                    sch.dma("pool", (lambda q, dg=dg, tok0=tok0: q.dma_start(
                        out=u_out[dg * 128:(dg + 1) * 128, tok0:tok0 + PT],
                        in_=s_ub[:])),
                        reads=["ub"])
                else:
                    dg = rt - 8
                    sch.op("act", (lambda s, pb=pb: s.activation(
                        out=s_zb[:], in_=p_in[pb][:], func=AF.Silu)),
                        reads=[pbuf], writes=["zb"])
                    sch.dma("pool", (lambda q, dg=dg, tok0=tok0: q.dma_start(
                        out=g_out[dg * 128:(dg + 1) * 128, tok0:tok0 + PT],
                        in_=s_zb[:])),
                        reads=["zb"])

        sch.op("pe", lambda p: p.nop(),
               extra=[sch.per["pool"][-1], sch.per["qld"][-1]])
        sch.emit(nc, sems)
    return nc


def build_l2():
    nc = bass.Bass("TRN2", target_bir_lowering=False, debug=False)
    u_full = nc.dram_tensor("u_full", [DI, L], BF16, kind="ExternalInput").ap()
    g_half = nc.dram_tensor("g_half", [DH, L], BF16, kind="ExternalInput").ap()
    w_xp_T = nc.dram_tensor("w_xp_T", [DI, 96], F32, kind="ExternalInput").ap()
    w_dt_T = nc.dram_tensor("w_dt_T", [DR, DH], F32, kind="ExternalInput").ap()
    b_dt_s = nc.dram_tensor("b_dt_s", [128, NDG], F32, kind="ExternalInput").ap()
    a_log_s = nc.dram_tensor("a_log_s", [128, NDG * DS], F32, kind="ExternalInput").ap()
    d_par_s = nc.dram_tensor("d_par_s", [128, NDG], F32, kind="ExternalInput").ap()
    yh_out = nc.dram_tensor("yh_out", [DH, L], BF16, kind="ExternalOutput").ap()
    bc_bf = nc.dram_tensor("bc_bf", [32, L], BF16, kind="Internal").ap()
    dt_d = nc.dram_tensor("dt_d", [64, L], BF16, kind="Internal").ap()

    sch = Sched()
    es = contextlib.ExitStack()
    with es:
        sb = lambda name, shape, dt=F32: es.enter_context(nc.sbuf_tensor(name, shape, dt))
        ps = lambda name, shape, dt=F32: es.enter_context(nc.psum_tensor(name, shape, dt))
        sems = {s: es.enter_context(nc.semaphore("sem_" + s)) for s in Sched.STREAMS}

        s_stage = sb("s_stage", [128, 1024])
        s_wxp_b = sb("s_wxp_b", [128, 16 * 96], BF16)
        s_wdt_b = sb("s_wdt_b", [64, DH], BF16)
        s_bdt = sb("s_bdt", [128, NDG])
        s_bdt80 = sb("s_bdt80", [128, NDG])
        s_A = sb("s_A", [128, NDG * DS])
        s_D = sb("s_D", [128, NDG])
        s_dt_b = sb("s_dt_b", [64, QF], BF16)
        s_state = sb("s_state", [128, DS * NDG])
        s_uk = sb("s_uk", [128, PT], BF16)
        s_fence2 = sb("s_fence2", [1, 16], BF16)
        s_fa2 = sb("s_fa2", [1, 16], BF16)
        s_bcb = sb("s_bcb", [32, PT], BF16)
        s_const = sb("s_const", [128, 2])
        nc.const_aps.register(F32, 1.0, s_const[:, 1:2])
        p_xp = ps("p_xp", [96, PT])
        p_dt = ps("p_dt", [128, QF])

        sch.op("dve", lambda v: v.memset(s_const[:, 1:2], 1.0), writes=["consts"])
        st0 = sch.op("dve", lambda v: v.memset(s_state[:], 0.0), writes=["state"])
        for col in range(DS * NDG):
            sch.set_writer("state%d" % col, st0)

        for i in range(16):
            sch.dma("qld", (lambda q, i=i: q.dma_start(
                out=s_stage[:, 0:96], in_=w_xp_T[i * 128:(i + 1) * 128, :])),
                reads=["stage"], writes=["stage"])
            sch.fence("qld", lambda q: q.dma_start(out=s_fence2[:], in_=u_full[0:1, 0:16]), ["stage"])
            sch.op("dve", (lambda v, i=i: v.tensor_copy(
                out=s_wxp_b[:, i * 96:(i + 1) * 96], in_=s_stage[:, 0:96])),
                reads=["stage"], writes=["wxp"])
        sch.dma("qld", lambda q: q.dma_start(out=s_stage[0:64, 0:DH], in_=w_dt_T),
                reads=["stage"], writes=["stage"])
        sch.fence("qld", lambda q: q.dma_start(out=s_fence2[:], in_=u_full[0:1, 0:16]), ["stage"])
        sch.op("dve", lambda v: v.tensor_copy(out=s_wdt_b[:], in_=s_stage[0:64, 0:DH]),
               reads=["stage"], writes=["wdt"])
        sch.dma("qld", lambda q: q.dma_start(out=s_bdt[:], in_=b_dt_s), writes=["bdt"])
        sch.dma("qld", lambda q: q.dma_start(out=s_A[:], in_=a_log_s), writes=["A"])
        sch.dma("qld", lambda q: q.dma_start(out=s_D[:], in_=d_par_s), writes=["Dp"])
        sch.fence("qld", lambda q: q.dma_start(out=s_fence2[:], in_=u_full[0:1, 0:16]),
                  ["bdt", "A", "Dp"])
        sch.op("act", lambda s: s.activation(out=s_A[:], in_=s_A[:], func=AF.Exp),
               reads=["A"], writes=["A"])
        sch.op("dve", lambda v: v.tensor_scalar(out=s_A[:], in0=s_A[:], scalar1=-1.0,
               scalar2=None, op0=OP.mult), reads=["A"], writes=["A"])
        sch.op("dve", lambda v: v.tensor_scalar(out=s_bdt80[:], in0=s_bdt[:],
               scalar1=-80.0, scalar2=None, op0=OP.add), reads=["bdt"], writes=["bdt80"])

        for t in range(NPT):
            tok0 = t * PT
            first = None
            for kt in range(16):
                fence = sch.dma("qld", (lambda q, kt=kt, tok0=tok0: q.dma_start(
                    out=s_uk[:],
                    in_=u_full[kt * 128:(kt + 1) * 128, tok0:tok0 + PT])),
                    writes=["uk0"])
                fence = sch.dma("qld", lambda q: q.dma_start(
                    out=s_fence2[:], in_=u_full[0:1, 0:16]))
                sch.set_writer("uk0", fence)
                for hf in range(2):
                    first = sch.op("pe", (lambda p, kt=kt, hf=hf: p.matmul(
                        out=p_xp[:, hf * 512:(hf + 1) * 512],
                        lhsT=s_wxp_b[:, kt * 96:(kt + 1) * 96],
                        rhs=s_uk[:, hf * 512:(hf + 1) * 512],
                        start=(kt == 0), stop=(kt == 15))),
                        reads=["uk0", "wxp"],
                        writes=(["pxp"] if kt == 0 and hf == 0 else []))
            gid = sch.op("pe", lambda p: p.drain(), reads=["uk0", "wxp"])
            sch.set_writer("pxp", gid)
            sch.op("act", (lambda s: s.copy(
                out=s_dt_b[:, 0:PT], in_=p_xp[0:64, :])),
                reads=["pxp"], writes=["dtb"])
            sch.dma("qld", (lambda q, tok0=tok0: q.dma_start(
                out=dt_d[:, tok0:tok0 + PT], in_=s_dt_b[:, 0:PT])),
                reads=["dtb"], writes=["dt_dd"])
            sch.op("act", (lambda s: s.copy(
                out=s_bcb[:], in_=p_xp[64:96, :])),
                reads=["pxp"], writes=["bcb"])
            sch.dma("qld", (lambda q, tok0=tok0: q.dma_start(
                out=bc_bf[:, tok0:tok0 + PT], in_=s_bcb[:])),
                reads=["bcb"], writes=["bc_d"])
            sch.fence("qld", lambda q: q.dma_start(out=s_fence2[:], in_=u_full[0:1, 0:16]), ["bc_d"])

        s_delta = sb("s_delta", [128, NDG * QF], BF16)
        s_g = sb("s_g", [128, NDG * QF], BF16)
        s_y = sb("s_y", [128, NDG * QF])
        s_brep = [sb("s_brep0", [128, QF], BF16), sb("s_brep1", [128, QF], BF16)]
        s_crep = [sb("s_crep0", [128, QF], BF16), sb("s_crep1", [128, QF], BF16)]
        s_dA = [sb("s_dA0", [128, QF]), sb("s_dA1", [128, QF])]
        s_w = [sb("s_w0", [128, QF], BF16), sb("s_w1", [128, QF], BF16)]
        s_h = [sb("s_h0", [128, QF], BF16), sb("s_h1", [128, QF], BF16)]
        s_t = [sb("s_t0b", [128, QF], BF16), sb("s_t1b", [128, QF], BF16)]

        for q_ in range(NQ):
            q0 = q_ * QF
            sch.dma("qld", (lambda q, q0=q0: q.dma_start(
                out=s_dt_b[:], in_=dt_d[:, q0:q0 + QF])),
                reads=["dt_dd", "dtb"], writes=["dtb"])
            fence = sch.dma("qld", lambda q: q.dma_start(
                out=s_fence2[:], in_=u_full[0:1, 0:16]))
            sch.set_writer("dtb", fence)
            for dg in range(NDG):
                first = None
                for j in range(4):
                    first = sch.op("pe", (lambda p, dg=dg, q0=q0, j=j: p.matmul(
                        out=p_dt[:, j * 512:(j + 1) * 512],
                        lhsT=s_wdt_b[:, dg * 128:(dg + 1) * 128],
                        rhs=s_dt_b[:, j * 512:(j + 1) * 512],
                        start=True, stop=True)),
                        reads=["dtb", "wdt"], writes=(["pdt"] if j == 0 else []))
                gid = sch.op("pe", lambda p: p.drain(), reads=["dtb", "wdt"])
                sch.set_writer("pdt", gid)
                sch.op("dve", (lambda v, dg=dg: v.tensor_scalar(
                    out=s_dA[0][:], in0=p_dt[:],
                    scalar1=s_bdt[:, dg:dg + 1], scalar2=80.0,
                    op0=OP.add, op1=OP.min)),
                    reads=["pdt", "bdt"], writes=["dA0"])
                sch.op("dve", (lambda v, dg=dg: v.tensor_scalar(
                    out=s_dA[1][:], in0=p_dt[:],
                    scalar1=s_bdt80[:, dg:dg + 1], scalar2=0.0,
                    op0=OP.add, op1=OP.max)),
                    reads=["pdt", "bdt80"], writes=["dA1"])
                sch.op("act", (lambda s: s.activation(
                    out=s_dA[0][:], in_=s_dA[0][:], func=AF.Exp)),
                    reads=["dA0"], writes=["dA0"])
                sch.op("act", (lambda s: s.activation(
                    out=s_dA[0][:], in_=s_dA[0][:], func=AF.Ln, bias=1.0)),
                    reads=["dA0", "consts"], writes=["dA0"])
                sch.op("dve", (lambda v, dg=dg: v.tensor_add(
                    out=s_delta[:, dg * QF:(dg + 1) * QF], in0=s_dA[0][:], in1=s_dA[1][:])),
                    reads=["dA0", "dA1"], writes=["delta%d" % dg])
                sch.dma("qld", (lambda q, dg=dg, q0=q0: q.dma_start(
                    out=s_w[1][:], in_=u_full[dg * 128:(dg + 1) * 128, q0:q0 + QF])),
                    writes=["w1"])
                sch.fence("qld", lambda q: q.dma_start(out=s_fence2[:], in_=u_full[0:1, 0:16]), ["w1"])
                sch.op("dve", (lambda v, dg=dg: v.tensor_mul(
                    out=s_g[:, dg * QF:(dg + 1) * QF],
                    in0=s_delta[:, dg * QF:(dg + 1) * QF], in1=s_w[1][:])),
                    reads=["delta%d" % dg, "w1"], writes=["g%d" % dg])
                sch.op("dve", (lambda v, dg=dg: v.tensor_scalar(
                    out=s_y[:, dg * QF:(dg + 1) * QF], in0=s_w[1][:],
                    scalar1=s_D[:, dg:dg + 1], scalar2=None, op0=OP.mult)),
                    reads=["w1", "Dp"], writes=["y%d" % dg])
            for n in range(DS):
                bs = n % 2
                sch.dma("act", (lambda q, n=n, q0=q0, bs=bs: q.dma_start(
                    out=s_brep[bs][:],
                    in_=bc_bf[n:n + 1, q0:q0 + QF].partition_broadcast(128))),
                    reads=["bc_d"], writes=["brep%d" % bs])
                sch.dma("act", (lambda q, n=n, q0=q0, bs=bs: q.dma_start(
                    out=s_crep[bs][:],
                    in_=bc_bf[16 + n:17 + n, q0:q0 + QF].partition_broadcast(128))),
                    reads=["bc_d"], writes=["crep%d" % bs])
                sch.fence("act", lambda q: q.dma_start(out=s_fa2[:], in_=u_full[0:1, 0:16]),
                          ["brep%d" % bs, "crep%d" % bs])
                for dg in range(NDG):
                    ds = (n * NDG + dg) % 2
                    col = n * NDG + dg
                    sch.op("act", (lambda s, dg=dg, n=n, ds=ds: s.activation(
                        out=s_dA[ds][:], in_=s_delta[:, dg * QF:(dg + 1) * QF],
                        func=AF.Exp, scale=s_A[:, dg * DS + n: dg * DS + n + 1])),
                        reads=["delta%d" % dg, "A"], writes=["dA%d" % ds])
                    sch.op("dve", (lambda v, dg=dg, ds=ds, bs=bs: v.tensor_mul(
                        out=s_w[ds][:], in0=s_g[:, dg * QF:(dg + 1) * QF],
                        in1=s_brep[bs][:])),
                        reads=["g%d" % dg, "brep%d" % bs], writes=["w%d" % ds])
                    sch.op("dve", (lambda v, ds=ds, col=col: v.tensor_tensor_scan(
                        out=s_h[ds][:], data0=s_dA[ds][:], data1=s_w[ds][:],
                        initial=s_state[:, col:col + 1],
                        op0=OP.mult, op1=OP.add)),
                        reads=["dA%d" % ds, "w%d" % ds, "state%d" % col],
                        writes=["h%d" % ds])
                    sch.op("dve", (lambda v, ds=ds, col=col: v.tensor_copy(
                        out=s_state[:, col:col + 1], in_=s_h[ds][:, QF - 1:QF])),
                        reads=["h%d" % ds], writes=["state%d" % col])
                    sch.op("dve", (lambda v, ds=ds, bs=bs: v.tensor_mul(
                        out=s_t[ds][:], in0=s_h[ds][:], in1=s_crep[bs][:])),
                        reads=["h%d" % ds, "crep%d" % bs], writes=["t%d" % ds])
                    sch.op("dve", (lambda v, dg=dg, ds=ds: v.tensor_add(
                        out=s_y[:, dg * QF:(dg + 1) * QF],
                        in0=s_y[:, dg * QF:(dg + 1) * QF], in1=s_t[ds][:])),
                        reads=["t%d" % ds, "y%d" % dg], writes=["y%d" % dg])
            for dg in range(NDG):
                sch.dma("qld", (lambda q, dg=dg, q0=q0: q.dma_start(
                    out=s_w[1][:], in_=g_half[dg * 128:(dg + 1) * 128, q0:q0 + QF])),
                    writes=["w1"])
                sch.fence("qld", lambda q: q.dma_start(out=s_fence2[:], in_=u_full[0:1, 0:16]), ["w1"])
                sch.op("dve", (lambda v, dg=dg: v.tensor_mul(
                    out=s_w[0][:], in0=s_y[:, dg * QF:(dg + 1) * QF], in1=s_w[1][:])),
                    reads=["y%d" % dg, "w1"], writes=["w0"])
                sch.dma("act", (lambda q, dg=dg, q0=q0: q.dma_start(
                    out=yh_out[dg * 128:(dg + 1) * 128, q0:q0 + QF], in_=s_w[0][:])),
                    reads=["w0"], writes=["yhs"])
        sch.op("pe", lambda p: p.nop(), extra=[sch.per["act"][-1]])
        sch.emit(nc, sems)
    return nc


def build_l3():
    nc = bass.Bass("TRN2", target_bir_lowering=False, debug=False)
    yh_tok = nc.dram_tensor("yh_tok", [DI, LH], BF16, kind="ExternalInput").ap()
    xq_tok = nc.dram_tensor("xq_tok", [DM, LH], BF16, kind="ExternalInput").ap()
    x_epi = nc.dram_tensor("x_epi", [DM, LH], F32, kind="ExternalInput").ap()
    xs_tok = nc.dram_tensor("xs_tok", [1, LH], F32, kind="ExternalInput").ap()
    w_out_T = nc.dram_tensor("w_out_T", [DI, DM], F32, kind="ExternalInput").ap()
    wq_T = nc.dram_tensor("wq_T", [DM, DM], I8, kind="ExternalInput").ap()
    wsc_s = nc.dram_tensor("wsc_s", [128, 8], F32, kind="ExternalInput").ap()
    fb_s = nc.dram_tensor("fb_s", [128, 8], F32, kind="ExternalInput").ap()
    out = nc.dram_tensor("out", [DM, LH], F32, kind="ExternalOutput").ap()

    sch = Sched()
    es = contextlib.ExitStack()
    with es:
        sb = lambda name, shape, dt=F32: es.enter_context(nc.sbuf_tensor(name, shape, dt))
        ps = lambda name, shape, dt=F32: es.enter_context(nc.psum_tensor(name, shape, dt))
        sems = {s: es.enter_context(nc.semaphore("sem_" + s)) for s in Sched.STREAMS}

        s_stage = sb("s_stage", [128, 1024])
        s_stage8 = sb("s_stage8", [128, 8 * DM], I8)
        s_wout_b = sb("s_wout_b", [128, 16 * DM], BF16)
        s_wq_b = sb("s_wq_b", [128, 8 * DM], BF16)
        s_wsc = sb("s_wsc", [128, 8])
        s_fb = sb("s_fb", [128, 8])
        s_yhk = [sb("s_yhk0", [128, 16 * 512], BF16), sb("s_yhk1", [128, 16 * 512], BF16)]
        s_xqk = [sb("s_xqk0", [128, 8 * 512], BF16), sb("s_xqk1", [128, 8 * 512], BF16)]
        s_xsr = [sb("s_xsr0", [128, 512]), sb("s_xsr1", [128, 512])]
        s_sv = sb("s_sv", [128, 8 * 512])
        s_fence = sb("s_fence", [1, 16], BF16)
        s_gv = sb("s_gv", [128, 512], BF16)
        s_xv = sb("s_xv", [128, 512])
        s_ov = sb("s_ov", [128, 512])
        s_ffm = sb("s_ffm", [128, 512])
        p_o = [ps("p_o0", [128, 512]), ps("p_o1", [128, 512])]
        p_f = [ps("p_f0", [128, 512]), ps("p_f1", [128, 512])]

        for i in range(16):
            sch.dma("qld", (lambda q, i=i: q.dma_start(
                out=s_stage[:], in_=w_out_T[i * 128:(i + 1) * 128, :])),
                reads=["stage"], writes=["stage"])
            sch.fence("qld", lambda q: q.dma_start(out=s_fence[:], in_=yh_tok[0:1, 0:16]), ["stage"])
            sch.op("dve", (lambda v, i=i: v.tensor_copy(
                out=s_wout_b[:, i * DM:(i + 1) * DM], in_=s_stage[:])),
                reads=["stage"], writes=["wout"])
        sch.dma("qld", lambda q: q.dma_start(
            out=s_stage8[:].rearrange("p (a b) -> p a b", b=DM),
            in_=wq_T.rearrange("(a p) b -> p a b", p=128)), writes=["stage8"])
        sch.fence("qld", lambda q: q.dma_start(out=s_fence[:], in_=yh_tok[0:1, 0:16]), ["stage8"])
        sch.op("dve", lambda v: v.tensor_copy(out=s_wq_b[:], in_=s_stage8[:]),
               reads=["stage8"], writes=["wq"])
        sch.dma("qld", lambda q: q.dma_start(out=s_wsc[:], in_=wsc_s), writes=["wsc"])
        sch.dma("qld", lambda q: q.dma_start(out=s_fb[:], in_=fb_s), writes=["fb"])
        sch.fence("qld", lambda q: q.dma_start(out=s_fence[:], in_=yh_tok[0:1, 0:16]),
                  ["wsc", "fb"])

        odmas = []
        for lc in range(LH // 512):
            l0 = lc * 512
            ksl = lc % 2
            for kt in range(16):
                sch.dma("qld", (lambda q, kt=kt, l0=l0, ksl=ksl: q.dma_start(
                    out=s_yhk[ksl][:, kt * 512:(kt + 1) * 512],
                    in_=yh_tok[kt * 128:(kt + 1) * 128, l0:l0 + 512])),
                    writes=["yhk%d_%d" % (ksl, kt)])
            for kt in range(8):
                sch.dma("qld", (lambda q, kt=kt, l0=l0, ksl=ksl: q.dma_start(
                    out=s_xqk[ksl][:, kt * 512:(kt + 1) * 512],
                    in_=xq_tok[kt * 128:(kt + 1) * 128, l0:l0 + 512])),
                    writes=["xqk%d_%d" % (ksl, kt)])
            sch.dma("qld", (lambda q, l0=l0, ksl=ksl: q.dma_start(
                out=s_xsr[ksl][:], in_=xs_tok[0:1, l0:l0 + 512].partition_broadcast(128))),
                writes=["xsr%d" % ksl])
            fence = sch.dma("qld", lambda q: q.dma_start(
                out=s_fence[:], in_=yh_tok[0:1, 0:16]))
            for k2 in range(16):
                sch.set_writer("yhk%d_%d" % (ksl, k2), fence)
            for k2 in range(8):
                sch.set_writer("xqk%d_%d" % (ksl, k2), fence)
            sch.set_writer("xsr%d" % ksl, fence)
            for ot in range(8):
                po = ot % 2
                first = None
                for kt in range(16):
                    first = sch.op("pe", (lambda p, ot=ot, kt=kt, po=po, ksl=ksl: p.matmul(
                        out=p_o[po][:],
                        lhsT=s_wout_b[:, kt * DM + ot * 128: kt * DM + (ot + 1) * 128],
                        rhs=s_yhk[ksl][:, kt * 512:(kt + 1) * 512],
                        start=(kt == 0), stop=(kt == 15))),
                        reads=["yhk%d_%d" % (ksl, kt), "wout"],
                        writes=(["po%d" % po] if kt == 0 else []))
                gid = sch.op("pe", (lambda p: p.drain()),
                             reads=["yhk%d_%d" % (ksl, k2) for k2 in range(16)] + ["wout"])
                sch.set_writer("po%d" % po, gid)
                sch.op("act", (lambda s, po=po, ot=ot: s.copy(
                    out=s_sv[:, ot * 512:(ot + 1) * 512], in_=p_o[po][:])),
                    reads=["po%d" % po], writes=["sv%d" % ot])
                firstf = None
                for kt in range(8):
                    firstf = sch.op("pe", (lambda p, ot=ot, kt=kt, po=po, ksl=ksl: p.matmul(
                        out=p_f[po][:],
                        lhsT=s_wq_b[:, kt * DM + ot * 128: kt * DM + (ot + 1) * 128],
                        rhs=s_xqk[ksl][:, kt * 512:(kt + 1) * 512],
                        start=(kt == 0), stop=(kt == 7))),
                        reads=["xqk%d_%d" % (ksl, kt), "wq"],
                        writes=(["pf%d" % po] if kt == 0 else []))
                gidf = sch.op("pe", (lambda p: p.drain()),
                              reads=["xqk%d_%d" % (ksl, k2) for k2 in range(8)] + ["wq"])
                sch.set_writer("pf%d" % po, gidf)
                sch.op("dve", (lambda v, po=po, ksl=ksl: v.tensor_mul(
                    out=s_ffm[:], in0=p_f[po][:], in1=s_xsr[ksl][:])),
                    reads=["pf%d" % po, "xsr%d" % ksl], writes=["ffm"])
                sch.op("act", (lambda s, ot=ot: s.activation(
                    out=s_gv[:], in_=s_ffm[:], func=AF.Silu,
                    scale=s_wsc[:, ot:ot + 1], bias=s_fb[:, ot:ot + 1])),
                    reads=["ffm", "wsc", "fb"], writes=["gv"])
                sch.dma("qld", (lambda q, ot=ot, l0=l0: q.dma_start(
                    out=s_xv[:], in_=x_epi[ot * 128:(ot + 1) * 128, l0:l0 + 512])),
                    reads=["xv"], writes=["xv"])
                sch.fence("qld", lambda q: q.dma_start(out=s_fence[:], in_=yh_tok[0:1, 0:16]), ["xv"])
                sch.op("dve", (lambda v, ot=ot: v.tensor_mul(
                    out=s_ov[:], in0=s_sv[:, ot * 512:(ot + 1) * 512], in1=s_gv[:])),
                    reads=["sv%d" % ot, "gv"], writes=["ov"])
                sch.op("dve", lambda v: v.tensor_add(out=s_ov[:], in0=s_ov[:], in1=s_xv[:]),
                       reads=["ov", "xv"], writes=["ov"])
                odmas.append(sch.dma("pool", (lambda q, ot=ot, l0=l0: q.dma_start(
                    out=out[ot * 128:(ot + 1) * 128, l0:l0 + 512], in_=s_ov[:])),
                    reads=["ov"]))
        sch.op("pe", lambda p: p.nop(), extra=odmas)
        sch.emit(nc, sems)
    return nc


_CACHE = {}


def kernel(L0_attn_hidden, L0_attn_in_proj_w, L0_attn_conv_w, L0_attn_conv_b,
           L0_attn_x_proj_w, L0_attn_dt_proj_w, L0_attn_dt_proj_b, L0_attn_A_log,
           L0_attn_D_param, L0_attn_out_proj_w, L0_ffn_weight_q,
           L0_ffn_weight_scale, L0_ffn_bias):
    x = np.asarray(L0_attn_hidden, dtype=np.float32)
    Win = np.asarray(L0_attn_in_proj_w, dtype=np.float32)
    convw = np.asarray(L0_attn_conv_w, dtype=np.float32)[:, 0, :]
    convb = np.asarray(L0_attn_conv_b, dtype=np.float32)
    Wxp = np.asarray(L0_attn_x_proj_w, dtype=np.float32)
    Wdt = np.asarray(L0_attn_dt_proj_w, dtype=np.float32)
    bdt = np.asarray(L0_attn_dt_proj_b, dtype=np.float32)
    Alog = np.asarray(L0_attn_A_log, dtype=np.float32)
    Dp = np.asarray(L0_attn_D_param, dtype=np.float32)
    Wout = np.asarray(L0_attn_out_proj_w, dtype=np.float32)
    Wq = np.asarray(L0_ffn_weight_q)
    wsc = np.asarray(L0_ffn_weight_scale, dtype=np.float32)
    fb = np.asarray(L0_ffn_bias, dtype=np.float32)
    C = np.ascontiguousarray

    if "l1" not in _CACHE:
        Sched.SERIALIZE = ("qld", "dve")
        _CACHE["l1"] = build_l1()
        Sched.SERIALIZE = ()
        _CACHE["l2"] = build_l2()
        Sched.SERIALIZE = ("qld", "dve")
        _CACHE["l3"] = build_l3()
        Sched.SERIALIZE = ()

    def shp(a, k):
        n = a.shape[0] // 128
        return C(a.reshape(n, 128, *a.shape[1:]).transpose(
            (1, 0, 2) if a.ndim == 2 else (1, 0)).reshape(128, -1))

    xT = [C(x[b].T) for b in range(B)]
    m1 = []
    for c in range(8):
        b, h = c // 2, c % 2
        dsl = slice(h * DH, (h + 1) * DH)
        rows = np.r_[h * DH:(h + 1) * DH, DI + h * DH: DI + (h + 1) * DH]
        m1.append({
            "x_tok": C(x[b]),
            "x_feat": xT[b],
            "w_in_T": C(Win[rows].T),
            "conv_w_s": shp(convw[dsl], KC),
            "conv_b_s": shp(convb[dsl], 1),
        })
    r1 = run_bass_kernel_spmd(_CACHE["l1"], m1, core_ids=list(range(8)))

    m2 = []
    for c in range(8):
        b, h = c // 2, c % 2
        dsl = slice(h * DH, (h + 1) * DH)
        own = r1.results[c]["u_out"]
        other = r1.results[2 * b + (1 - h)]["u_out"]
        u_full = np.concatenate([own, other], axis=0)
        wxp_rows = np.r_[h * DH:(h + 1) * DH, (1 - h) * DH:(2 - h) * DH]
        m2.append({
            "u_full": C(u_full),
            "g_half": r1.results[c]["g_out"],
            "w_xp_T": C(Wxp[:, :].T[wxp_rows]),
            "w_dt_T": C(Wdt[dsl].T),
            "b_dt_s": shp(bdt[dsl], 1),
            "a_log_s": shp(Alog[dsl], DS),
            "d_par_s": shp(Dp[dsl], 1),
        })
    r2 = run_bass_kernel_spmd(_CACHE["l2"], m2, core_ids=list(range(8)))

    m3 = []
    for c in range(8):
        b, h = c // 2, c % 2
        tsl = slice(h * LH, (h + 1) * LH)
        yh_full = np.concatenate([r2.results[2 * b]["yh_out"],
                                  r2.results[2 * b + 1]["yh_out"]], axis=0)
        m3.append({
            "yh_tok": C(yh_full[:, tsl]),
            "xq_tok": C(r1.results[c]["xq_out"][:, tsl]),
            "x_epi": C(xT[b][:, tsl]),
            "xs_tok": C(r1.results[c]["xs_out"][:, tsl]),
            "w_out_T": C(Wout.T),
            "wq_T": C(Wq.T),
            "wsc_s": shp(wsc, 1),
            "fb_s": shp(fb, 1),
        })
    r3 = run_bass_kernel_spmd(_CACHE["l3"], m3, core_ids=list(range(8)))

    outf = np.empty((B, L, DM), dtype=np.float32)
    for c in range(8):
        b, h = c // 2, c % 2
        outf[b, h * LH:(h + 1) * LH, :] = r3.results[c]["out"].T
    return outf




# version 08
# speedup vs baseline: 1.2166x; 1.0157x over previous
import contextlib
import numpy as np
import concourse.bass as bass
import concourse.mybir as mybir
from concourse.bass_utils import run_bass_kernel_spmd

F32 = mybir.dt.float32
BF16 = mybir.dt.bfloat16
I8 = mybir.dt.int8
AF = mybir.ActivationFunctionType
OP = mybir.AluOpType
AX = mybir.AxisListType

B, L, DM = 4, 8192, 1024
DI, DS, DR, KC = 2048, 16, 64, 4
DH = DI // 2
NDG = DH // 128
PT = 1024
NPT = L // PT
QF = 2048
NQ = L // QF
LH = L // 2


class Sched:
    STREAMS = ("pe", "act", "dve", "pool", "qld")

    def __init__(self):
        self.ops = []
        self.per = {s: [] for s in self.STREAMS}
        self.cum = {}
        self.stream_of = {}
        self.bw = {}
        self.br = {}

    SERIALIZE = ()

    def op(self, stream, fn, reads=(), writes=(), extra=(), inc=1):
        deps = [d for d in extra if d is not None]
        if (stream in Sched.SERIALIZE or "all" in Sched.SERIALIZE) and self.ops:
            deps.append(len(self.ops) - 1)
        for b in reads:
            deps += self.bw.get(b, [])
        for b in writes:
            deps += self.bw.get(b, []) + self.br.get(b, [])
        oid = len(self.ops)
        self.ops.append((fn, deps, inc))
        prev = self.per[stream][-1] if self.per[stream] else None
        self.cum[oid] = (self.cum[prev] if prev is not None else 0) + inc
        self.per[stream].append(oid)
        self.stream_of[oid] = stream
        for b in reads:
            self.br.setdefault(b, []).append(oid)
        for b in writes:
            self.bw[b] = [oid]
            self.br[b] = []
        return oid

    def dma(self, stream, fn, reads=(), writes=(), extra=()):
        return self.op(stream, fn, reads, writes, extra, inc=16)

    def set_writer(self, buf, oid):
        self.bw[buf] = [oid]
        self.br[buf] = []

    def set_fence(self, src_ap, dram_ap):
        self._fsrc = src_ap
        self._fdram = dram_ap
        self._fidx = 0

    def fence(self, stream, fn, bufs):
        i = self._fidx
        self._fidx += 1
        src, dst = self._fsrc, self._fdram
        oid = self.dma(stream, lambda q, i=i: q.dma_start(
            out=dst[0:1, i * 16:(i + 1) * 16], in_=src))
        for b in bufs:
            self.set_writer(b, oid)
        return oid

    def emit(self, nc, sems):
        def emit_stream(stream, eng):
            hw = {s: 0 for s in self.STREAMS}
            for oid in self.per[stream]:
                fn, deps, inc = self.ops[oid]
                need = {}
                for d in deps:
                    ds = self.stream_of[d]
                    v = self.cum[d]
                    if ds == stream and v >= self.cum[oid]:
                        continue
                    if ds == stream and inc == 16 and self.ops[d][2] == 16:
                        continue
                    if v > hw[ds]:
                        need[ds] = max(need.get(ds, 0), v)
                for ds in sorted(need):
                    eng.wait_ge(sems[ds], need[ds])
                    hw[ds] = need[ds]
                fn(eng).then_inc(sems[stream], inc)

        with nc.Block() as block:
            @block.tensor
            def _(eng):
                emit_stream("pe", eng)

            @block.scalar
            def _(eng):
                emit_stream("act", eng)

            @block.vector
            def _(eng):
                emit_stream("dve", eng)

            @block.gpsimd
            def _(eng):
                emit_stream("pool", eng)

            @block.sync
            def _(eng):
                emit_stream("qld", eng)


def build_l1():
    nc = bass.Bass("TRN2", target_bir_lowering=False, debug=False)
    x_tok = nc.dram_tensor("x_tok", [L, DM], F32, kind="ExternalInput").ap()
    x_feat = nc.dram_tensor("x_feat", [DM, L], F32, kind="ExternalInput").ap()
    w_in_T = nc.dram_tensor("w_in_T", [DM, 2 * DH], F32, kind="ExternalInput").ap()
    conv_w_s = nc.dram_tensor("conv_w_s", [128, NDG * KC], F32, kind="ExternalInput").ap()
    conv_b_s = nc.dram_tensor("conv_b_s", [128, NDG], F32, kind="ExternalInput").ap()
    u_out = nc.dram_tensor("u_out", [DH, L], BF16, kind="ExternalOutput").ap()
    g_out = nc.dram_tensor("g_out", [DH, L], BF16, kind="ExternalOutput").ap()
    xq_out = nc.dram_tensor("xq_out", [DM, L], BF16, kind="ExternalOutput").ap()
    xs_out = nc.dram_tensor("xs_out", [1, L], F32, kind="ExternalOutput").ap()
    dr = lambda n, s, dt=F32: nc.dram_tensor(n, s, dt, kind="Internal").ap()
    st_nmr = dr("st_nmr", [1, L])
    st_rst = dr("st_rst", [1, L])
    st_ixs = dr("st_ixs", [1, L])

    sch = Sched()
    es = contextlib.ExitStack()
    with es:
        sb = lambda name, shape, dt=F32: es.enter_context(nc.sbuf_tensor(name, shape, dt))
        ps = lambda name, shape, dt=F32: es.enter_context(nc.psum_tensor(name, shape, dt))
        sems = {s: es.enter_context(nc.semaphore("sem_" + s)) for s in Sched.STREAMS}
        fence_d0 = nc.dram_tensor("fence_d", [1, 16 * 4096], F32, kind="Internal").ap()
        s_fsrc0 = sb("s_fsrc", [1, 16])
        sch.set_fence(s_fsrc0[0:1, :], fence_d0)

        s_stage = sb("s_stage", [128, 2048])
        s_win_b = sb("s_win_b", [128, 8 * 2048], BF16)
        s_convw = sb("s_convw", [128, NDG * KC])
        s_convb = sb("s_convb", [128, NDG])
        s_xt = sb("s_xt", [128, 2 * DM])
        s_sum = sb("s_sum", [128, 8])
        s_sumsq = sb("s_sumsq", [128, 8])
        s_mx = sb("s_mx", [128, 8])
        s_mn = sb("s_mn", [128, 8])
        s_m = sb("s_m", [128, 8])
        s_var = sb("s_var", [128, 8])
        s_rstd = sb("s_rstd", [128, 8])
        s_nmr = sb("s_nmr", [128, 8])
        s_amx = sb("s_amx", [128, 8])
        s_xs = sb("s_xs", [128, 8])
        s_trash = sb("s_trash", [128, DM])
        s_xf = sb("s_xf", [128, 2 * PT])
        s_mrep = sb("s_mrep", [128, PT])
        s_rrep = sb("s_rrep", [128, PT])
        s_qrep = sb("s_qrep", [128, PT])
        s_t0 = sb("s_t0", [128, PT])
        s_s1 = sb("s_s1", [128, PT])
        s_xq8 = sb("s_xq8", [128, PT], I8)
        s_xnb = sb("s_xnb", [128, 8 * PT], BF16)
        s_xqb = sb("s_xqb", [128, PT], BF16)
        s_xpe = sb("s_xpe", [128, PT + 3])
        s_halo = sb("s_halo", [128, NDG * 3])
        s_conv = sb("s_conv", [128, PT])
        s_ub = sb("s_ub", [128, PT], BF16)
        s_zb = sb("s_zb", [128, PT], BF16)
        s_const = sb("s_const", [128, 2])
        s_fb1 = sb("s_fb1", [1, 16])
        s_fp1 = sb("s_fp1", [1, 16])
        nc.const_aps.register(F32, 1e-5, s_const[:, 0:1])
        p_in = [ps("p_in0", [128, PT]), ps("p_in1", [128, PT])]

        sch.op("dve", lambda v: v.memset(s_const[:, 0:1], 1e-5), writes=["consts"])
        sch.op("dve", lambda v: v.memset(s_halo[:], 0.0), writes=["halo"])

        for i in range(8):
            sch.dma("qld", (lambda q, i=i: q.dma_start(
                out=s_stage[:], in_=w_in_T[i * 128:(i + 1) * 128, :])),
                reads=["stage"], writes=["stage"])
            sch.fence("qld", lambda q: q.dma_start(out=s_fb1[:], in_=x_tok[0:1, 0:16]), ["stage"])
            sch.op("dve", (lambda v, i=i: v.tensor_copy(
                out=s_win_b[:, i * 2048:(i + 1) * 2048], in_=s_stage[:])),
                reads=["stage"], writes=["win"])
        sch.dma("qld", lambda q: q.dma_start(out=s_convw[:], in_=conv_w_s), writes=["convw"])
        sch.dma("qld", lambda q: q.dma_start(out=s_convb[:], in_=conv_b_s), writes=["convb"])
        sch.fence("qld", lambda q: q.dma_start(out=s_fb1[:], in_=x_tok[0:1, 0:16]),
                  ["convw", "convb"])

        for t in range(NPT):
            tok0 = t * PT
            for ssub in range(8):
                slot = ssub % 2
                xts = s_xt[:, slot * DM:(slot + 1) * DM]
                xb = "xt%d" % slot
                sch.dma("qld", (lambda q, t=t, ssub=ssub, xts=xts: q.dma_start(
                    out=xts, in_=x_tok[(t * 8 + ssub) * 128:(t * 8 + ssub + 1) * 128, :])),
                    writes=[xb])
                sch.fence("qld", lambda q: q.dma_start(out=s_fb1[:], in_=x_tok[0:1, 0:16]), [xb])
                sch.op("dve", (lambda v, xts=xts, ssub=ssub: v.tensor_reduce(
                    out=s_sum[:, ssub:ssub + 1], in_=xts, axis=AX.X, op=OP.add)),
                    reads=[xb], writes=["sum"])
                sch.op("act", (lambda s, xts=xts: s.activation(
                    out=s_trash[:], in_=xts, func=AF.Square)),
                    reads=[xb], writes=["trash"])
                sch.op("dve", (lambda v, ssub=ssub: v.tensor_reduce(
                    out=s_sumsq[:, ssub:ssub + 1], in_=s_trash[:], axis=AX.X, op=OP.add)),
                    reads=["trash"], writes=["sumsq"])
                sch.op("dve", (lambda v, xts=xts, ssub=ssub: v.tensor_reduce(
                    out=s_mx[:, ssub:ssub + 1], in_=xts, axis=AX.X, op=OP.max)),
                    reads=[xb], writes=["mx"])
                sch.op("dve", (lambda v, xts=xts, ssub=ssub: v.tensor_reduce(
                    out=s_mn[:, ssub:ssub + 1], in_=xts, axis=AX.X, op=OP.min)),
                    reads=[xb], writes=["mn"])
            sch.op("dve", lambda v: v.tensor_scalar(out=s_m[:], in0=s_sum[:],
                   scalar1=1.0 / DM, scalar2=None, op0=OP.mult),
                   reads=["sum"], writes=["m"])
            sch.op("dve", lambda v: v.tensor_tensor(out=s_var[:], in0=s_m[:],
                   in1=s_m[:], op=OP.mult), reads=["m"], writes=["var"])
            sch.op("dve", lambda v: v.scalar_tensor_tensor(out=s_var[:], in0=s_sumsq[:],
                   scalar=1.0 / DM, in1=s_var[:], op0=OP.mult, op1=OP.subtract),
                   reads=["sumsq", "var"], writes=["var"])
            sch.op("act", lambda s: s.activation(out=s_rstd[:], in_=s_var[:],
                   func=AF.Sqrt, bias=1e-5), reads=["var", "consts"], writes=["rstd"])
            sch.op("dve", lambda v: v.reciprocal(out=s_rstd[:], in_=s_rstd[:]),
                   reads=["rstd"], writes=["rstd"])
            sch.op("dve", lambda v: v.scalar_tensor_tensor(out=s_nmr[:], in0=s_m[:],
                   scalar=-1.0, in1=s_rstd[:], op0=OP.mult, op1=OP.mult),
                   reads=["m", "rstd"], writes=["nmr"])
            sch.op("dve", lambda v: v.tensor_sub(out=s_mx[:], in0=s_mx[:], in1=s_m[:]),
                   reads=["mx", "m"], writes=["mx"])
            sch.op("dve", lambda v: v.tensor_sub(out=s_mn[:], in0=s_m[:], in1=s_mn[:]),
                   reads=["mn", "m"], writes=["mn"])
            sch.op("dve", lambda v: v.tensor_max(out=s_amx[:], in0=s_mx[:], in1=s_mn[:]),
                   reads=["mx", "mn"], writes=["amx"])
            sch.op("dve", lambda v: v.tensor_tensor(out=s_amx[:], in0=s_amx[:],
                   in1=s_rstd[:], op=OP.mult), reads=["amx", "rstd"], writes=["amx"])
            sch.op("dve", lambda v: v.tensor_scalar(out=s_xs[:], in0=s_amx[:],
                   scalar1=1e-6, scalar2=1.0 / 127.0, op0=OP.max, op1=OP.mult),
                   reads=["amx"], writes=["xs"])
            sch.op("dve", lambda v: v.reciprocal(out=s_amx[:], in_=s_xs[:]),
                   reads=["xs"], writes=["amx"])
            for ssub in range(8):
                o0 = tok0 + ssub * 128
                sch.dma("pool", (lambda q, ssub=ssub, o0=o0: q.dma_start(
                    out=st_nmr[0:1, o0:o0 + 128], in_=s_nmr[:, ssub:ssub + 1])),
                    reads=["nmr"], writes=["st_nmr"])
                sch.dma("pool", (lambda q, ssub=ssub, o0=o0: q.dma_start(
                    out=st_rst[0:1, o0:o0 + 128], in_=s_rstd[:, ssub:ssub + 1])),
                    reads=["rstd"], writes=["st_rst"])
                sch.dma("pool", (lambda q, ssub=ssub, o0=o0: q.dma_start(
                    out=st_ixs[0:1, o0:o0 + 128], in_=s_amx[:, ssub:ssub + 1])),
                    reads=["amx"], writes=["st_ixs"])
                sch.dma("pool", (lambda q, ssub=ssub, o0=o0: q.dma_start(
                    out=xs_out[0:1, o0:o0 + 128], in_=s_xs[:, ssub:ssub + 1])),
                    reads=["xs"], writes=["xs_d"])

            sch.fence("pool", lambda q: q.dma_start(out=s_fp1[:], in_=st_nmr[0:1, 0:16]),
                      ["st_nmr", "st_rst", "st_ixs", "xs_d"])
            sch.dma("qld", (lambda q, tok0=tok0: q.dma_start(
                out=s_mrep[:], in_=st_nmr[0:1, tok0:tok0 + PT].partition_broadcast(128))),
                reads=["st_nmr"], writes=["mrep"])
            sch.dma("qld", (lambda q, tok0=tok0: q.dma_start(
                out=s_rrep[:], in_=st_rst[0:1, tok0:tok0 + PT].partition_broadcast(128))),
                reads=["st_rst"], writes=["rrep"])
            sch.dma("qld", (lambda q, tok0=tok0: q.dma_start(
                out=s_qrep[:], in_=st_ixs[0:1, tok0:tok0 + PT].partition_broadcast(128))),
                reads=["st_ixs"], writes=["qrep"])
            for dmt in range(8):
                xsl = dmt % 2
                xfb = "xf%d" % xsl
                sch.dma("qld", (lambda q, dmt=dmt, tok0=tok0, xsl=xsl: q.dma_start(
                    out=s_xf[:, xsl * PT:(xsl + 1) * PT],
                    in_=x_feat[dmt * 128:(dmt + 1) * 128, tok0:tok0 + PT])),
                    writes=[xfb])
                sch.fence("qld", lambda q: q.dma_start(out=s_fb1[:], in_=x_tok[0:1, 0:16]), [xfb])
                sch.op("dve", (lambda v, xsl=xsl: v.tensor_mul(
                    out=s_t0[:], in0=s_xf[:, xsl * PT:(xsl + 1) * PT], in1=s_rrep[:])),
                    reads=[xfb, "rrep"], writes=["t0"])
                sch.op("dve", (lambda v: v.tensor_add(
                    out=s_s1[:], in0=s_t0[:], in1=s_mrep[:])),
                    reads=["t0", "mrep"], writes=["s1"])
                sch.op("dve", (lambda v, dmt=dmt: v.tensor_copy(
                    out=s_xnb[:, dmt * PT:(dmt + 1) * PT], in_=s_s1[:])),
                    reads=["s1"], writes=["xnb%d" % dmt])
                sch.op("dve", (lambda v: v.tensor_mul(
                    out=s_xq8[:], in0=s_s1[:], in1=s_qrep[:])),
                    reads=["s1", "qrep"], writes=["xq8"])
                sch.op("act", (lambda s: s.copy(
                    out=s_xqb[:], in_=s_xq8[:])),
                    reads=["xq8"], writes=["xqb"])
                sch.dma("pool", (lambda q, dmt=dmt, tok0=tok0: q.dma_start(
                    out=xq_out[dmt * 128:(dmt + 1) * 128, tok0:tok0 + PT], in_=s_xqb[:])),
                    reads=["xqb"])

            for rt in range(16):
                pb = rt % 2
                pbuf = "pin%d" % pb
                first = None
                for kt in range(8):
                    for hf in range(2):
                        first = sch.op("pe", (lambda p, rt=rt, kt=kt, pb=pb, hf=hf: p.matmul(
                            out=p_in[pb][:, hf * 512:(hf + 1) * 512],
                            lhsT=s_win_b[:, kt * 2048 + rt * 128: kt * 2048 + (rt + 1) * 128],
                            rhs=s_xnb[:, kt * PT + hf * 512: kt * PT + (hf + 1) * 512],
                            start=(kt == 0), stop=(kt == 7))),
                            reads=["xnb%d" % kt, "win"],
                            writes=([pbuf] if kt == 0 and hf == 0 else []))
                gid = sch.op("pe", lambda p: p.drain(),
                             reads=["xnb%d" % k2 for k2 in range(8)] + ["win"])
                sch.set_writer(pbuf, gid)
                if rt < 8:
                    dg = rt
                    sch.op("dve", (lambda v, pb=pb: v.tensor_copy(
                        out=s_xpe[:, 3:PT + 3], in_=p_in[pb][:])),
                        reads=[pbuf], writes=["xpe"])
                    sch.op("dve", (lambda v, dg=dg: v.tensor_copy(
                        out=s_xpe[:, 0:3], in_=s_halo[:, dg * 3:dg * 3 + 3])),
                        reads=["halo"], writes=["xpe_h"])
                    sch.op("dve", (lambda v, dg=dg: v.tensor_copy(
                        out=s_halo[:, dg * 3:dg * 3 + 3], in_=s_xpe[:, PT:PT + 3])),
                        reads=["xpe"], writes=["halo"])
                    sch.op("dve", (lambda v, dg=dg: v.tensor_scalar(
                        out=s_conv[:], in0=s_xpe[:, 0:PT],
                        scalar1=s_convw[:, dg * KC + 0: dg * KC + 1], scalar2=None,
                        op0=OP.mult)),
                        reads=["xpe", "xpe_h", "convw"], writes=["conv"])
                    sch.op("dve", (lambda v, dg=dg: v.scalar_tensor_tensor(
                        out=s_conv[:], in0=s_xpe[:, 1:PT + 1],
                        scalar=s_convw[:, dg * KC + 1: dg * KC + 2], in1=s_conv[:],
                        op0=OP.mult, op1=OP.add)),
                        reads=["xpe", "xpe_h", "conv", "convw"], writes=["conv"])
                    sch.op("dve", (lambda v, dg=dg: v.scalar_tensor_tensor(
                        out=s_conv[:], in0=s_xpe[:, 2:PT + 2],
                        scalar=s_convw[:, dg * KC + 2: dg * KC + 3], in1=s_conv[:],
                        op0=OP.mult, op1=OP.add)),
                        reads=["xpe", "xpe_h", "conv", "convw"], writes=["conv"])
                    sch.op("dve", (lambda v, dg=dg: v.scalar_tensor_tensor(
                        out=s_conv[:], in0=s_xpe[:, 3:PT + 3],
                        scalar=s_convw[:, dg * KC + 3: dg * KC + 4], in1=s_conv[:],
                        op0=OP.mult, op1=OP.add)),
                        reads=["xpe", "conv", "convw"], writes=["conv"])
                    sch.op("act", (lambda s, dg=dg: s.activation(
                        out=s_ub[:], in_=s_conv[:],
                        func=AF.Silu, bias=s_convb[:, dg:dg + 1])),
                        reads=["conv", "convb"], writes=["ub"])
                    sch.dma("pool", (lambda q, dg=dg, tok0=tok0: q.dma_start(
                        out=u_out[dg * 128:(dg + 1) * 128, tok0:tok0 + PT],
                        in_=s_ub[:])),
                        reads=["ub"])
                else:
                    dg = rt - 8
                    sch.op("act", (lambda s, pb=pb: s.activation(
                        out=s_zb[:], in_=p_in[pb][:], func=AF.Silu)),
                        reads=[pbuf], writes=["zb"])
                    sch.dma("pool", (lambda q, dg=dg, tok0=tok0: q.dma_start(
                        out=g_out[dg * 128:(dg + 1) * 128, tok0:tok0 + PT],
                        in_=s_zb[:])),
                        reads=["zb"])

        sch.op("pe", lambda p: p.nop(),
               extra=[sch.per["pool"][-1], sch.per["qld"][-1]])
        sch.emit(nc, sems)
    return nc


def build_l2():
    nc = bass.Bass("TRN2", target_bir_lowering=False, debug=False)
    u_full = nc.dram_tensor("u_full", [DI, L], BF16, kind="ExternalInput").ap()
    g_half = nc.dram_tensor("g_half", [DH, L], BF16, kind="ExternalInput").ap()
    w_xp_T = nc.dram_tensor("w_xp_T", [DI, 96], F32, kind="ExternalInput").ap()
    w_dt_T = nc.dram_tensor("w_dt_T", [DR, DH], F32, kind="ExternalInput").ap()
    b_dt_s = nc.dram_tensor("b_dt_s", [128, NDG], F32, kind="ExternalInput").ap()
    a_log_s = nc.dram_tensor("a_log_s", [128, NDG * DS], F32, kind="ExternalInput").ap()
    d_par_s = nc.dram_tensor("d_par_s", [128, NDG], F32, kind="ExternalInput").ap()
    yh_out = nc.dram_tensor("yh_out", [DH, L], BF16, kind="ExternalOutput").ap()
    bc_bf = nc.dram_tensor("bc_bf", [32, L], BF16, kind="Internal").ap()
    dt_d = nc.dram_tensor("dt_d", [64, L], BF16, kind="Internal").ap()

    sch = Sched()
    es = contextlib.ExitStack()
    with es:
        sb = lambda name, shape, dt=F32: es.enter_context(nc.sbuf_tensor(name, shape, dt))
        ps = lambda name, shape, dt=F32: es.enter_context(nc.psum_tensor(name, shape, dt))
        sems = {s: es.enter_context(nc.semaphore("sem_" + s)) for s in Sched.STREAMS}
        fence_d1 = nc.dram_tensor("fence_d", [1, 16 * 4096], F32, kind="Internal").ap()
        s_fsrc1 = sb("s_fsrc", [1, 16])
        sch.set_fence(s_fsrc1[0:1, :], fence_d1)

        s_stage = sb("s_stage", [128, 1024])
        s_wxp_b = sb("s_wxp_b", [128, 16 * 96], BF16)
        s_wdt_b = sb("s_wdt_b", [64, DH], BF16)
        s_bdt = sb("s_bdt", [128, NDG])
        s_bdt80 = sb("s_bdt80", [128, NDG])
        s_A = sb("s_A", [128, NDG * DS])
        s_D = sb("s_D", [128, NDG])
        s_dt_b = sb("s_dt_b", [64, QF], BF16)
        s_state = sb("s_state", [128, DS * NDG])
        s_uk = sb("s_uk", [128, PT], BF16)
        s_fence2 = sb("s_fence2", [1, 16], BF16)
        s_fa2 = sb("s_fa2", [1, 16], BF16)
        s_bcb = sb("s_bcb", [32, PT], BF16)
        s_const = sb("s_const", [128, 2])
        nc.const_aps.register(F32, 1.0, s_const[:, 1:2])
        p_xp = ps("p_xp", [96, PT])
        p_dt = ps("p_dt", [128, QF])

        sch.op("dve", lambda v: v.memset(s_const[:, 1:2], 1.0), writes=["consts"])
        st0 = sch.op("dve", lambda v: v.memset(s_state[:], 0.0), writes=["state"])
        for col in range(DS * NDG):
            sch.set_writer("state%d" % col, st0)

        for i in range(16):
            sch.dma("qld", (lambda q, i=i: q.dma_start(
                out=s_stage[:, 0:96], in_=w_xp_T[i * 128:(i + 1) * 128, :])),
                reads=["stage"], writes=["stage"])
            sch.fence("qld", lambda q: q.dma_start(out=s_fence2[:], in_=u_full[0:1, 0:16]), ["stage"])
            sch.op("dve", (lambda v, i=i: v.tensor_copy(
                out=s_wxp_b[:, i * 96:(i + 1) * 96], in_=s_stage[:, 0:96])),
                reads=["stage"], writes=["wxp"])
        sch.dma("qld", lambda q: q.dma_start(out=s_stage[0:64, 0:DH], in_=w_dt_T),
                reads=["stage"], writes=["stage"])
        sch.fence("qld", lambda q: q.dma_start(out=s_fence2[:], in_=u_full[0:1, 0:16]), ["stage"])
        sch.op("dve", lambda v: v.tensor_copy(out=s_wdt_b[:], in_=s_stage[0:64, 0:DH]),
               reads=["stage"], writes=["wdt"])
        sch.dma("qld", lambda q: q.dma_start(out=s_bdt[:], in_=b_dt_s), writes=["bdt"])
        sch.dma("qld", lambda q: q.dma_start(out=s_A[:], in_=a_log_s), writes=["A"])
        sch.dma("qld", lambda q: q.dma_start(out=s_D[:], in_=d_par_s), writes=["Dp"])
        sch.fence("qld", lambda q: q.dma_start(out=s_fence2[:], in_=u_full[0:1, 0:16]),
                  ["bdt", "A", "Dp"])
        sch.op("act", lambda s: s.activation(out=s_A[:], in_=s_A[:], func=AF.Exp),
               reads=["A"], writes=["A"])
        sch.op("dve", lambda v: v.tensor_scalar(out=s_A[:], in0=s_A[:], scalar1=-1.0,
               scalar2=None, op0=OP.mult), reads=["A"], writes=["A"])
        sch.op("dve", lambda v: v.tensor_scalar(out=s_bdt80[:], in0=s_bdt[:],
               scalar1=-80.0, scalar2=None, op0=OP.add), reads=["bdt"], writes=["bdt80"])

        for t in range(NPT):
            tok0 = t * PT
            first = None
            for kt in range(16):
                fence = sch.dma("qld", (lambda q, kt=kt, tok0=tok0: q.dma_start(
                    out=s_uk[:],
                    in_=u_full[kt * 128:(kt + 1) * 128, tok0:tok0 + PT])),
                    writes=["uk0"])
                fence = sch.dma("qld", lambda q: q.dma_start(
                    out=s_fence2[:], in_=u_full[0:1, 0:16]))
                sch.set_writer("uk0", fence)
                for hf in range(2):
                    first = sch.op("pe", (lambda p, kt=kt, hf=hf: p.matmul(
                        out=p_xp[:, hf * 512:(hf + 1) * 512],
                        lhsT=s_wxp_b[:, kt * 96:(kt + 1) * 96],
                        rhs=s_uk[:, hf * 512:(hf + 1) * 512],
                        start=(kt == 0), stop=(kt == 15))),
                        reads=["uk0", "wxp"],
                        writes=(["pxp"] if kt == 0 and hf == 0 else []))
            gid = sch.op("pe", lambda p: p.drain(), reads=["uk0", "wxp"])
            sch.set_writer("pxp", gid)
            sch.op("act", (lambda s: s.copy(
                out=s_dt_b[:, 0:PT], in_=p_xp[0:64, :])),
                reads=["pxp"], writes=["dtb"])
            sch.dma("qld", (lambda q, tok0=tok0: q.dma_start(
                out=dt_d[:, tok0:tok0 + PT], in_=s_dt_b[:, 0:PT])),
                reads=["dtb"], writes=["dt_dd"])
            sch.op("act", (lambda s: s.copy(
                out=s_bcb[:], in_=p_xp[64:96, :])),
                reads=["pxp"], writes=["bcb"])
            sch.dma("qld", (lambda q, tok0=tok0: q.dma_start(
                out=bc_bf[:, tok0:tok0 + PT], in_=s_bcb[:])),
                reads=["bcb"], writes=["bc_d"])
            sch.fence("qld", lambda q: q.dma_start(out=s_fence2[:], in_=u_full[0:1, 0:16]), ["bc_d"])

        s_delta = sb("s_delta", [128, NDG * QF], BF16)
        s_g = sb("s_g", [128, NDG * QF], BF16)
        s_y = sb("s_y", [128, NDG * QF])
        s_brep = [sb("s_brep0", [128, QF], BF16), sb("s_brep1", [128, QF], BF16)]
        s_crep = [sb("s_crep0", [128, QF], BF16), sb("s_crep1", [128, QF], BF16)]
        s_dA = [sb("s_dA0", [128, QF]), sb("s_dA1", [128, QF])]
        s_w = [sb("s_w0", [128, QF], BF16), sb("s_w1", [128, QF], BF16)]
        s_h = [sb("s_h0", [128, QF], BF16), sb("s_h1", [128, QF], BF16)]
        s_t = [sb("s_t0b", [128, QF], BF16), sb("s_t1b", [128, QF], BF16)]

        for q_ in range(NQ):
            q0 = q_ * QF
            sch.dma("qld", (lambda q, q0=q0: q.dma_start(
                out=s_dt_b[:], in_=dt_d[:, q0:q0 + QF])),
                reads=["dt_dd", "dtb"], writes=["dtb"])
            fence = sch.dma("qld", lambda q: q.dma_start(
                out=s_fence2[:], in_=u_full[0:1, 0:16]))
            sch.set_writer("dtb", fence)
            for dg in range(NDG):
                first = None
                for j in range(4):
                    first = sch.op("pe", (lambda p, dg=dg, q0=q0, j=j: p.matmul(
                        out=p_dt[:, j * 512:(j + 1) * 512],
                        lhsT=s_wdt_b[:, dg * 128:(dg + 1) * 128],
                        rhs=s_dt_b[:, j * 512:(j + 1) * 512],
                        start=True, stop=True)),
                        reads=["dtb", "wdt"], writes=(["pdt"] if j == 0 else []))
                gid = sch.op("pe", lambda p: p.drain(), reads=["dtb", "wdt"])
                sch.set_writer("pdt", gid)
                sch.op("dve", (lambda v, dg=dg: v.tensor_scalar(
                    out=s_dA[0][:], in0=p_dt[:],
                    scalar1=s_bdt[:, dg:dg + 1], scalar2=80.0,
                    op0=OP.add, op1=OP.min)),
                    reads=["pdt", "bdt"], writes=["dA0"])
                sch.op("dve", (lambda v, dg=dg: v.tensor_scalar(
                    out=s_dA[1][:], in0=p_dt[:],
                    scalar1=s_bdt80[:, dg:dg + 1], scalar2=0.0,
                    op0=OP.add, op1=OP.max)),
                    reads=["pdt", "bdt80"], writes=["dA1"])
                sch.op("act", (lambda s: s.activation(
                    out=s_dA[0][:], in_=s_dA[0][:], func=AF.Exp)),
                    reads=["dA0"], writes=["dA0"])
                sch.op("act", (lambda s: s.activation(
                    out=s_dA[0][:], in_=s_dA[0][:], func=AF.Ln, bias=1.0)),
                    reads=["dA0", "consts"], writes=["dA0"])
                sch.op("dve", (lambda v, dg=dg: v.tensor_add(
                    out=s_delta[:, dg * QF:(dg + 1) * QF], in0=s_dA[0][:], in1=s_dA[1][:])),
                    reads=["dA0", "dA1"], writes=["delta%d" % dg])
                sch.dma("qld", (lambda q, dg=dg, q0=q0: q.dma_start(
                    out=s_w[1][:], in_=u_full[dg * 128:(dg + 1) * 128, q0:q0 + QF])),
                    writes=["w1"])
                sch.fence("qld", lambda q: q.dma_start(out=s_fence2[:], in_=u_full[0:1, 0:16]), ["w1"])
                sch.op("dve", (lambda v, dg=dg: v.tensor_mul(
                    out=s_g[:, dg * QF:(dg + 1) * QF],
                    in0=s_delta[:, dg * QF:(dg + 1) * QF], in1=s_w[1][:])),
                    reads=["delta%d" % dg, "w1"], writes=["g%d" % dg])
                sch.op("dve", (lambda v, dg=dg: v.tensor_scalar(
                    out=s_y[:, dg * QF:(dg + 1) * QF], in0=s_w[1][:],
                    scalar1=s_D[:, dg:dg + 1], scalar2=None, op0=OP.mult)),
                    reads=["w1", "Dp"], writes=["y%d" % dg])
            for n in range(DS):
                bs = n % 2
                sch.dma("act", (lambda q, n=n, q0=q0, bs=bs: q.dma_start(
                    out=s_brep[bs][:],
                    in_=bc_bf[n:n + 1, q0:q0 + QF].partition_broadcast(128))),
                    reads=["bc_d"], writes=["brep%d" % bs])
                sch.dma("act", (lambda q, n=n, q0=q0, bs=bs: q.dma_start(
                    out=s_crep[bs][:],
                    in_=bc_bf[16 + n:17 + n, q0:q0 + QF].partition_broadcast(128))),
                    reads=["bc_d"], writes=["crep%d" % bs])
                sch.fence("act", lambda q: q.dma_start(out=s_fa2[:], in_=u_full[0:1, 0:16]),
                          ["brep%d" % bs, "crep%d" % bs])
                for dg in range(NDG):
                    ds = (n * NDG + dg) % 2
                    col = n * NDG + dg
                    sch.op("act", (lambda s, dg=dg, n=n, ds=ds: s.activation(
                        out=s_dA[ds][:], in_=s_delta[:, dg * QF:(dg + 1) * QF],
                        func=AF.Exp, scale=s_A[:, dg * DS + n: dg * DS + n + 1])),
                        reads=["delta%d" % dg, "A"], writes=["dA%d" % ds])
                    sch.op("dve", (lambda v, dg=dg, ds=ds, bs=bs: v.tensor_mul(
                        out=s_w[ds][:], in0=s_g[:, dg * QF:(dg + 1) * QF],
                        in1=s_brep[bs][:])),
                        reads=["g%d" % dg, "brep%d" % bs], writes=["w%d" % ds])
                    sch.op("dve", (lambda v, ds=ds, col=col: v.tensor_tensor_scan(
                        out=s_h[ds][:], data0=s_dA[ds][:], data1=s_w[ds][:],
                        initial=s_state[:, col:col + 1],
                        op0=OP.mult, op1=OP.add)),
                        reads=["dA%d" % ds, "w%d" % ds, "state%d" % col],
                        writes=["h%d" % ds])
                    sch.op("dve", (lambda v, ds=ds, col=col: v.tensor_copy(
                        out=s_state[:, col:col + 1], in_=s_h[ds][:, QF - 1:QF])),
                        reads=["h%d" % ds], writes=["state%d" % col])
                    sch.op("dve", (lambda v, ds=ds, bs=bs: v.tensor_mul(
                        out=s_t[ds][:], in0=s_h[ds][:], in1=s_crep[bs][:])),
                        reads=["h%d" % ds, "crep%d" % bs], writes=["t%d" % ds])
                    sch.op("dve", (lambda v, dg=dg, ds=ds: v.tensor_add(
                        out=s_y[:, dg * QF:(dg + 1) * QF],
                        in0=s_y[:, dg * QF:(dg + 1) * QF], in1=s_t[ds][:])),
                        reads=["t%d" % ds, "y%d" % dg], writes=["y%d" % dg])
            for dg in range(NDG):
                sch.dma("qld", (lambda q, dg=dg, q0=q0: q.dma_start(
                    out=s_w[1][:], in_=g_half[dg * 128:(dg + 1) * 128, q0:q0 + QF])),
                    writes=["w1"])
                sch.fence("qld", lambda q: q.dma_start(out=s_fence2[:], in_=u_full[0:1, 0:16]), ["w1"])
                sch.op("dve", (lambda v, dg=dg: v.tensor_mul(
                    out=s_w[0][:], in0=s_y[:, dg * QF:(dg + 1) * QF], in1=s_w[1][:])),
                    reads=["y%d" % dg, "w1"], writes=["w0"])
                sch.dma("act", (lambda q, dg=dg, q0=q0: q.dma_start(
                    out=yh_out[dg * 128:(dg + 1) * 128, q0:q0 + QF], in_=s_w[0][:])),
                    reads=["w0"], writes=["yhs"])
        sch.op("pe", lambda p: p.nop(), extra=[sch.per["act"][-1]])
        sch.emit(nc, sems)
    return nc


def build_l3():
    nc = bass.Bass("TRN2", target_bir_lowering=False, debug=False)
    yh_tok = nc.dram_tensor("yh_tok", [DI, LH], BF16, kind="ExternalInput").ap()
    xq_tok = nc.dram_tensor("xq_tok", [DM, LH], BF16, kind="ExternalInput").ap()
    x_epi = nc.dram_tensor("x_epi", [DM, LH], F32, kind="ExternalInput").ap()
    xs_tok = nc.dram_tensor("xs_tok", [1, LH], F32, kind="ExternalInput").ap()
    w_out_T = nc.dram_tensor("w_out_T", [DI, DM], F32, kind="ExternalInput").ap()
    wq_T = nc.dram_tensor("wq_T", [DM, DM], I8, kind="ExternalInput").ap()
    wsc_s = nc.dram_tensor("wsc_s", [128, 8], F32, kind="ExternalInput").ap()
    fb_s = nc.dram_tensor("fb_s", [128, 8], F32, kind="ExternalInput").ap()
    out = nc.dram_tensor("out", [DM, LH], F32, kind="ExternalOutput").ap()

    sch = Sched()
    es = contextlib.ExitStack()
    with es:
        sb = lambda name, shape, dt=F32: es.enter_context(nc.sbuf_tensor(name, shape, dt))
        ps = lambda name, shape, dt=F32: es.enter_context(nc.psum_tensor(name, shape, dt))
        sems = {s: es.enter_context(nc.semaphore("sem_" + s)) for s in Sched.STREAMS}
        fence_d2 = nc.dram_tensor("fence_d", [1, 16 * 4096], F32, kind="Internal").ap()
        s_fsrc2 = sb("s_fsrc", [1, 16])
        sch.set_fence(s_fsrc2[0:1, :], fence_d2)

        s_stage = sb("s_stage", [128, 1024])
        s_stage8 = sb("s_stage8", [128, 8 * DM], I8)
        s_wout_b = sb("s_wout_b", [128, 16 * DM], BF16)
        s_wq_b = sb("s_wq_b", [128, 8 * DM], BF16)
        s_wsc = sb("s_wsc", [128, 8])
        s_fb = sb("s_fb", [128, 8])
        s_yhk = [sb("s_yhk0", [128, 16 * 512], BF16), sb("s_yhk1", [128, 16 * 512], BF16)]
        s_xqk = [sb("s_xqk0", [128, 8 * 512], BF16), sb("s_xqk1", [128, 8 * 512], BF16)]
        s_xsr = [sb("s_xsr0", [128, 512]), sb("s_xsr1", [128, 512])]
        s_sv = sb("s_sv", [128, 8 * 512])
        s_fence = sb("s_fence", [1, 16], BF16)
        s_gv = sb("s_gv", [128, 512], BF16)
        s_xv = sb("s_xv", [128, 512])
        s_ov = sb("s_ov", [128, 512])
        s_ffm = sb("s_ffm", [128, 512])
        p_o = [ps("p_o0", [128, 512]), ps("p_o1", [128, 512])]
        p_f = [ps("p_f0", [128, 512]), ps("p_f1", [128, 512])]

        for i in range(16):
            sch.dma("qld", (lambda q, i=i: q.dma_start(
                out=s_stage[:], in_=w_out_T[i * 128:(i + 1) * 128, :])),
                reads=["stage"], writes=["stage"])
            sch.fence("qld", lambda q: q.dma_start(out=s_fence[:], in_=yh_tok[0:1, 0:16]), ["stage"])
            sch.op("dve", (lambda v, i=i: v.tensor_copy(
                out=s_wout_b[:, i * DM:(i + 1) * DM], in_=s_stage[:])),
                reads=["stage"], writes=["wout"])
        sch.dma("qld", lambda q: q.dma_start(
            out=s_stage8[:].rearrange("p (a b) -> p a b", b=DM),
            in_=wq_T.rearrange("(a p) b -> p a b", p=128)), writes=["stage8"])
        sch.fence("qld", lambda q: q.dma_start(out=s_fence[:], in_=yh_tok[0:1, 0:16]), ["stage8"])
        sch.op("dve", lambda v: v.tensor_copy(out=s_wq_b[:], in_=s_stage8[:]),
               reads=["stage8"], writes=["wq"])
        sch.dma("qld", lambda q: q.dma_start(out=s_wsc[:], in_=wsc_s), writes=["wsc"])
        sch.dma("qld", lambda q: q.dma_start(out=s_fb[:], in_=fb_s), writes=["fb"])
        sch.fence("qld", lambda q: q.dma_start(out=s_fence[:], in_=yh_tok[0:1, 0:16]),
                  ["wsc", "fb"])

        odmas = []
        for lc in range(LH // 512):
            l0 = lc * 512
            ksl = lc % 2
            for kt in range(16):
                sch.dma("qld", (lambda q, kt=kt, l0=l0, ksl=ksl: q.dma_start(
                    out=s_yhk[ksl][:, kt * 512:(kt + 1) * 512],
                    in_=yh_tok[kt * 128:(kt + 1) * 128, l0:l0 + 512])),
                    writes=["yhk%d_%d" % (ksl, kt)])
            for kt in range(8):
                sch.dma("qld", (lambda q, kt=kt, l0=l0, ksl=ksl: q.dma_start(
                    out=s_xqk[ksl][:, kt * 512:(kt + 1) * 512],
                    in_=xq_tok[kt * 128:(kt + 1) * 128, l0:l0 + 512])),
                    writes=["xqk%d_%d" % (ksl, kt)])
            sch.dma("qld", (lambda q, l0=l0, ksl=ksl: q.dma_start(
                out=s_xsr[ksl][:], in_=xs_tok[0:1, l0:l0 + 512].partition_broadcast(128))),
                writes=["xsr%d" % ksl])
            fence = sch.dma("qld", lambda q: q.dma_start(
                out=s_fence[:], in_=yh_tok[0:1, 0:16]))
            for k2 in range(16):
                sch.set_writer("yhk%d_%d" % (ksl, k2), fence)
            for k2 in range(8):
                sch.set_writer("xqk%d_%d" % (ksl, k2), fence)
            sch.set_writer("xsr%d" % ksl, fence)
            for ot in range(8):
                po = ot % 2
                first = None
                for kt in range(16):
                    first = sch.op("pe", (lambda p, ot=ot, kt=kt, po=po, ksl=ksl: p.matmul(
                        out=p_o[po][:],
                        lhsT=s_wout_b[:, kt * DM + ot * 128: kt * DM + (ot + 1) * 128],
                        rhs=s_yhk[ksl][:, kt * 512:(kt + 1) * 512],
                        start=(kt == 0), stop=(kt == 15))),
                        reads=["yhk%d_%d" % (ksl, kt), "wout"],
                        writes=(["po%d" % po] if kt == 0 else []))
                gid = sch.op("pe", (lambda p: p.drain()),
                             reads=["yhk%d_%d" % (ksl, k2) for k2 in range(16)] + ["wout"])
                sch.set_writer("po%d" % po, gid)
                sch.op("act", (lambda s, po=po, ot=ot: s.copy(
                    out=s_sv[:, ot * 512:(ot + 1) * 512], in_=p_o[po][:])),
                    reads=["po%d" % po], writes=["sv%d" % ot])
                firstf = None
                for kt in range(8):
                    firstf = sch.op("pe", (lambda p, ot=ot, kt=kt, po=po, ksl=ksl: p.matmul(
                        out=p_f[po][:],
                        lhsT=s_wq_b[:, kt * DM + ot * 128: kt * DM + (ot + 1) * 128],
                        rhs=s_xqk[ksl][:, kt * 512:(kt + 1) * 512],
                        start=(kt == 0), stop=(kt == 7))),
                        reads=["xqk%d_%d" % (ksl, kt), "wq"],
                        writes=(["pf%d" % po] if kt == 0 else []))
                gidf = sch.op("pe", (lambda p: p.drain()),
                              reads=["xqk%d_%d" % (ksl, k2) for k2 in range(8)] + ["wq"])
                sch.set_writer("pf%d" % po, gidf)
                sch.op("dve", (lambda v, po=po, ksl=ksl: v.tensor_mul(
                    out=s_ffm[:], in0=p_f[po][:], in1=s_xsr[ksl][:])),
                    reads=["pf%d" % po, "xsr%d" % ksl], writes=["ffm"])
                sch.op("act", (lambda s, ot=ot: s.activation(
                    out=s_gv[:], in_=s_ffm[:], func=AF.Silu,
                    scale=s_wsc[:, ot:ot + 1], bias=s_fb[:, ot:ot + 1])),
                    reads=["ffm", "wsc", "fb"], writes=["gv"])
                sch.dma("qld", (lambda q, ot=ot, l0=l0: q.dma_start(
                    out=s_xv[:], in_=x_epi[ot * 128:(ot + 1) * 128, l0:l0 + 512])),
                    reads=["xv"], writes=["xv"])
                sch.fence("qld", lambda q: q.dma_start(out=s_fence[:], in_=yh_tok[0:1, 0:16]), ["xv"])
                sch.op("dve", (lambda v, ot=ot: v.tensor_mul(
                    out=s_ov[:], in0=s_sv[:, ot * 512:(ot + 1) * 512], in1=s_gv[:])),
                    reads=["sv%d" % ot, "gv"], writes=["ov"])
                sch.op("dve", lambda v: v.tensor_add(out=s_ov[:], in0=s_ov[:], in1=s_xv[:]),
                       reads=["ov", "xv"], writes=["ov"])
                odmas.append(sch.dma("pool", (lambda q, ot=ot, l0=l0: q.dma_start(
                    out=out[ot * 128:(ot + 1) * 128, l0:l0 + 512], in_=s_ov[:])),
                    reads=["ov"]))
        sch.op("pe", lambda p: p.nop(), extra=odmas)
        sch.emit(nc, sems)
    return nc


_CACHE = {}


def kernel(L0_attn_hidden, L0_attn_in_proj_w, L0_attn_conv_w, L0_attn_conv_b,
           L0_attn_x_proj_w, L0_attn_dt_proj_w, L0_attn_dt_proj_b, L0_attn_A_log,
           L0_attn_D_param, L0_attn_out_proj_w, L0_ffn_weight_q,
           L0_ffn_weight_scale, L0_ffn_bias):
    x = np.asarray(L0_attn_hidden, dtype=np.float32)
    Win = np.asarray(L0_attn_in_proj_w, dtype=np.float32)
    convw = np.asarray(L0_attn_conv_w, dtype=np.float32)[:, 0, :]
    convb = np.asarray(L0_attn_conv_b, dtype=np.float32)
    Wxp = np.asarray(L0_attn_x_proj_w, dtype=np.float32)
    Wdt = np.asarray(L0_attn_dt_proj_w, dtype=np.float32)
    bdt = np.asarray(L0_attn_dt_proj_b, dtype=np.float32)
    Alog = np.asarray(L0_attn_A_log, dtype=np.float32)
    Dp = np.asarray(L0_attn_D_param, dtype=np.float32)
    Wout = np.asarray(L0_attn_out_proj_w, dtype=np.float32)
    Wq = np.asarray(L0_ffn_weight_q)
    wsc = np.asarray(L0_ffn_weight_scale, dtype=np.float32)
    fb = np.asarray(L0_ffn_bias, dtype=np.float32)
    C = np.ascontiguousarray

    if "l1" not in _CACHE:
        Sched.SERIALIZE = ("qld", "dve")
        _CACHE["l1"] = build_l1()
        Sched.SERIALIZE = ()
        _CACHE["l2"] = build_l2()
        Sched.SERIALIZE = ("qld", "dve")
        _CACHE["l3"] = build_l3()
        Sched.SERIALIZE = ()

    def shp(a, k):
        n = a.shape[0] // 128
        return C(a.reshape(n, 128, *a.shape[1:]).transpose(
            (1, 0, 2) if a.ndim == 2 else (1, 0)).reshape(128, -1))

    xT = [C(x[b].T) for b in range(B)]
    m1 = []
    for c in range(8):
        b, h = c // 2, c % 2
        dsl = slice(h * DH, (h + 1) * DH)
        rows = np.r_[h * DH:(h + 1) * DH, DI + h * DH: DI + (h + 1) * DH]
        m1.append({
            "x_tok": C(x[b]),
            "x_feat": xT[b],
            "w_in_T": C(Win[rows].T),
            "conv_w_s": shp(convw[dsl], KC),
            "conv_b_s": shp(convb[dsl], 1),
        })
    r1 = run_bass_kernel_spmd(_CACHE["l1"], m1, core_ids=list(range(8)))

    m2 = []
    for c in range(8):
        b, h = c // 2, c % 2
        dsl = slice(h * DH, (h + 1) * DH)
        own = r1.results[c]["u_out"]
        other = r1.results[2 * b + (1 - h)]["u_out"]
        u_full = np.concatenate([own, other], axis=0)
        wxp_rows = np.r_[h * DH:(h + 1) * DH, (1 - h) * DH:(2 - h) * DH]
        m2.append({
            "u_full": C(u_full),
            "g_half": r1.results[c]["g_out"],
            "w_xp_T": C(Wxp[:, :].T[wxp_rows]),
            "w_dt_T": C(Wdt[dsl].T),
            "b_dt_s": shp(bdt[dsl], 1),
            "a_log_s": shp(Alog[dsl], DS),
            "d_par_s": shp(Dp[dsl], 1),
        })
    r2 = run_bass_kernel_spmd(_CACHE["l2"], m2, core_ids=list(range(8)))

    m3 = []
    for c in range(8):
        b, h = c // 2, c % 2
        tsl = slice(h * LH, (h + 1) * LH)
        yh_full = np.concatenate([r2.results[2 * b]["yh_out"],
                                  r2.results[2 * b + 1]["yh_out"]], axis=0)
        m3.append({
            "yh_tok": C(yh_full[:, tsl]),
            "xq_tok": C(r1.results[c]["xq_out"][:, tsl]),
            "x_epi": C(xT[b][:, tsl]),
            "xs_tok": C(r1.results[c]["xs_out"][:, tsl]),
            "w_out_T": C(Wout.T),
            "wq_T": C(Wq.T),
            "wsc_s": shp(wsc, 1),
            "fb_s": shp(fb, 1),
        })
    r3 = run_bass_kernel_spmd(_CACHE["l3"], m3, core_ids=list(range(8)))

    outf = np.empty((B, L, DM), dtype=np.float32)
    for c in range(8):
        b, h = c // 2, c % 2
        outf[b, h * LH:(h + 1) * LH, :] = r3.results[c]["out"].T
    return outf


